# version 10
# baseline (speedup 1.0000x reference)
import numpy as np
from contextlib import ExitStack

import concourse.bass as bass
import concourse.mybir as mybir
from concourse.bass_utils import run_bass_kernel_spmd

F32 = mybir.dt.float32
BF16 = mybir.dt.bfloat16
AF = mybir.ActivationFunctionType
ALU = mybir.AluOpType
AX = mybir.AxisListType

T = 2048
D = 1024
H = 16
DH = 64
E = 16
CAP = 256
FEXP = 2048
FSH = 2048
HID1 = 4096
DC = D // 128
TC = T // 128
NT = T // 512
EPS = 1e-5

ENGINES = ("tensor", "vector", "scalar", "sync", "gpsimd")


class B:

    def __init__(self, nc):
        self.nc = nc
        self.plan = {e: [] for e in ENGINES}
        self.seq = {e: 0 for e in ENGINES}
        self.drain_at = {e: -1 for e in ENGINES}
        self.sems = {}
        self.semval = {}
        self.hw = {}
        self.ctx = ExitStack()

    def semaphore(self, name):
        h = self.ctx.enter_context(self.nc.semaphore(name))
        self.sems[name] = h
        self.semval[name] = 0
        return name

    def emit(self, eng, fn, inc=None, by=1):
        self.seq[eng] += 1
        if inc is not None:
            self.semval[inc] += by
            val = self.semval[inc]
            sem = self.sems[inc]
            self.plan[eng].append(
                lambda e, fn=fn, sem=sem, by=by: fn(e).then_inc(sem, by))
            return (inc, val)
        self.plan[eng].append(fn)
        return None

    def wait(self, eng, tok):
        if tok is None:
            return
        name, val = tok
        if self.hw.get((eng, name), 0) >= val:
            return
        self.hw[(eng, name)] = val
        sem = self.sems[name]
        self.seq[eng] += 1
        self.plan[eng].append(lambda e, sem=sem, val=val: e.wait_ge(sem, val))

    def drain(self, eng):
        if self.drain_at[eng] >= self.seq[eng]:
            return
        self.drain_at[eng] = self.seq[eng] + 1
        self.seq[eng] += 1
        self.plan[eng].append(lambda e: e.drain())

    def join(self, engines=("tensor", "vector", "scalar")):
        toks = {}
        for eng in engines:
            sem = "J" + eng[0].upper()
            if sem not in self.sems:
                self.semaphore(sem)
            toks[eng] = self.emit(eng, lambda e: e.drain(), inc=sem)
            self.drain_at[eng] = self.seq[eng]
        for eng in engines:
            for other, tok in toks.items():
                if other != eng:
                    self.wait(eng, tok)
        self.last_join = toks
        return toks

    def gp_fence(self):
        for tok in getattr(self, "last_join", {}).values():
            self.wait("gpsimd", tok)

    def replay(self):
        nc = self.nc
        with nc.Block() as block:
            @block.tensor
            def _(t):
                for fn in self.plan["tensor"]:
                    fn(t)

            @block.vector
            def _(v):
                for fn in self.plan["vector"]:
                    fn(v)

            @block.scalar
            def _(s):
                for fn in self.plan["scalar"]:
                    fn(s)

            @block.sync
            def _(s):
                for fn in self.plan["sync"]:
                    fn(s)

            @block.gpsimd
            def _(g):
                for fn in self.plan["gpsimd"]:
                    fn(g)


def ts(i, n=128):
    return slice(i * n, (i + 1) * n)


class Prog:
    def __init__(self, upto="all", dbg=None, nexp=E):
        self.upto = upto
        self.dbg = dbg
        self.nexp = nexp
        import os as _os
        self.nc = bass.Bass("TRN2", target_bir_lowering=False,
                            detect_race_conditions=bool(_os.environ.get("RACEDET")))
        self.dram = {}
        self.b = B(self.nc)

    def din(self, name, shape):
        self.dram[name] = self.nc.dram_tensor(
            name, list(shape), F32, kind="ExternalInput").ap()

    def av(self, byte_off, parts, cols, dt=BF16):
        assert byte_off % 32 == 0
        esz = 2 if dt == BF16 else 4
        assert byte_off + cols * esz <= self.arena_bytes, (byte_off, cols, dt)
        full = self.AR[:].bitcast(dt) if dt != F32 else self.AR[:]
        e0 = byte_off // esz
        return full[0:parts, e0:e0 + cols]

    def bank(self, i, parts=128, n=1, dt=F32, p0=0):
        if dt == F32:
            return self.PS[p0:p0 + parts, i * 512:(i + n) * 512]
        full = self.PS[:].bitcast(BF16)
        return full[p0:p0 + parts, i * 1024:(i + n) * 1024]

    def build(self):
        nc, b = self.nc, self.b
        for name, shape in [
            ("xT", (128, DC * T)), ("cs_tab", (128, TC * DH)), ("ident", (128, 128)),
            ("wqkv_s", (12, 128, 2048)), ("bqkv", (1, 3 * D)),
            ("wout0", (DC, 128, D)), ("bout0_pt", (128, DC)),
            ("gate_pk", (128, 128)), ("lt128", (128, 128)),
            ("wg_s", (E, 4, 128, 4096)),
            ("wu_s", (E, 4, 128, 4096)),
            ("wd_s", (E, 4, 128, 4096)),
            ("wsg_s", (4, 128, 4096)),
            ("wsu_s", (4, 128, 4096)),
            ("wsd_s", (FSH // 128, 128, D)),
            ("wq1_s", (12, 128, 2048)), ("bqkv1", (1, 3 * D)),
            ("wout1", (DC, 128, D)), ("bout1_pt", (128, DC)),
            ("w1_s", (HID1 // 128, 128, D)), ("b1_pt", (128, HID1 // 128)),
            ("w2_s", (DC, 128, HID1)), ("b2_pt", (128, DC)),
        ]:
            self.din(name, shape)
        self.out = nc.dram_tensor("outT", [D, T], F32, kind="ExternalOutput").ap()
        self.dbg_t = None
        if self.dbg is not None:
            self.dbg_t = nc.dram_tensor("dbg", [128, 16384], F32,
                                        kind="ExternalOutput").ap()

        ctx = self.b.ctx
        with ctx:
            s = self

            def sbuf(name, shape, dt=BF16):
                return ctx.enter_context(nc.sbuf_tensor(name, list(shape), dt))

            s.x_all = sbuf("x_all", (128, DC * T), F32)
            s.x_fm = [s.x_all[:, i * T:(i + 1) * T] for i in range(DC)]
            s.wbuf = [sbuf(f"wbuf{i}", (128, 2048), BF16) for i in range(2)]
            s.ident = sbuf("id", (128, 128), BF16)
            s.ones = sbuf("one", (128, 128), BF16)
            s.rows = sbuf("rows", (1, 1024), BF16)
            s.mrow = s.rows[:, 0:512]
            s.rrow = s.rows[:, 512:1024]
            s.bias_sl = [s.rows[:, 0:256], s.rows[:, 256:512]]
            s.slots = sbuf("slots", (1, 72), F32)
            s.mneg_col = sbuf("mnegc", (128, 1), F32)
            s.bcol = sbuf("bcol", (128, DC), F32)
            s.b1col = sbuf("b1col", (128, HID1 // 128), F32)
            s.b2col = sbuf("b2col", (128, DC), F32)
            s.a1col = sbuf("a1col", (128, HID1 // 128), F32)
            s.a2col = sbuf("a2col", (128, DC), F32)
            s.m_cols = sbuf("mcols", (128, TC), F32)
            s.r_cols = sbuf("rcols", (128, TC), F32)
            s.ones_f = sbuf("onesf", (128, 128), F32)
            s.ident_f = sbuf("idf", (128, 128), F32)
            rem = nc.sbuf_bytes_remaining - 64
            s.AR = sbuf("arena", (128, rem // 4), F32)
            s.arena_bytes = (rem // 4) * 4
            s.PS = ctx.enter_context(nc.psum_tensor("ps", [128, 4096], F32))

            s.LDX = b.semaphore("LDX")
            s.LDW = b.semaphore("LDW")
            s.ST = b.semaphore("ST")
            s.TS = b.semaphore("TS")
            s.VS = b.semaphore("VS")
            s.SS = b.semaphore("SS")
            s.STG = b.semaphore("STG")
            s.GS = b.semaphore("GS")
            s.SCONST = b.semaphore("SCONST")
            s.SA = b.semaphore("SA")
            s.SB = b.semaphore("SB")
            s.SA2 = b.semaphore("SA2")
            s.SB2 = b.semaphore("SB2")
            s.SD = [b.semaphore("SD0"), b.semaphore("SD1")]
            s.SW = [b.semaphore("SW0"), b.semaphore("SW1")]
            s.SBIA = [b.semaphore("SBIA0"), b.semaphore("SBIA1")]
            s.SWOUT = b.semaphore("SWOUT")

            self.emit_phases()
        return nc

    def emit_phases(self):
        b, s = self.b, self
        dram = self.dram

        A_XLN = 0
        A_K = 32 * 1024
        A_V = 64 * 1024
        A_Q = A_V + TC * (H * (DH + 1)) * 2
        A_CS = A_Q + 32 * 1024
        A_STG = A_CS + 2 * 1024
        assert A_STG + 1024 <= s.arena_bytes, (A_STG, s.arena_bytes)
        s.xln = [s.av(A_XLN + i * 4096, 128, T) for i in range(DC)]
        s.k_fm = [s.av(A_K + i * 4096, 128, T) for i in range(DC)]
        s.v_tm = [s.av(A_V + i * 2 * H * (DH + 1), 128, H * (DH + 1)) for i in range(TC)]
        s.q_fm = [s.av(A_Q + i * 4096, 128, T) for i in range(DC)]
        s.cs_cos = s.av(A_CS, 128, TC * 32)
        s.cs_sin = s.av(A_CS + TC * 32 * 2, 128, TC * 32)
        s.stage = [s.av(A_STG + i * 512, 128, 256) for i in range(2)]
        s.ohx = s.av(0, 128, T)
        s.outh = [s.ohx] + s.q_fm[:DC - 1]
        s.p_t = [s.av(4 * 1024 + i * 2048, 128, 1024) for i in range(4)]
        s.p_t2 = [s.av(4 * 1024 + i * 4096, 128, 2048) for i in range(2)]
        s.recip_row = s.av(12 * 1024, 1, 1024)
        s.rbc = s.av(30 * 1024, 128, 1024)
        s.wout_sb = [s.av(14 * 1024 + i * 2048, 128, 1024) for i in range(DC)]
        s.xbf_st = [s.av(A_K + i * 1024, 128, 512) for i in range(2)]
        s.xsq_st = [s.av(A_K + 2048 + i * 1024, 128, 512) for i in range(2)]
        s.mr_row = s.av(A_K + 4096, 1, 512, F32)

        xt0 = b.emit("sync", lambda e: e.dma_start(
            out=s.x_all[:], in_=dram["xT"]), inc=s.LDX, by=16)
        x_toks = [xt0]
        b.wait("gpsimd", (s.SCONST, b.semval[s.SCONST]))
        ctok = b.emit("gpsimd", lambda e: e.dma_start(
            out=s.ident[:], in_=dram["ident"]), inc=s.SCONST, by=16)
        b.wait("sync", xt0)
        ctok2 = b.emit("sync", lambda e: e.dma_start(
            out=s.ident_f[:], in_=dram["ident"]), inc=s.LDX, by=16)
        s._idf_tok = ctok2
        b.emit("vector", lambda e: e.memset(s.ones[:], 1.0))
        b.emit("vector", lambda e: e.memset(s.ones_f[:], 1.0))
        b.drain("vector")
        for tok in x_toks:
            b.wait("vector", tok)
            b.wait("tensor", tok)
        b.wait("tensor", ctok)
        b.wait("vector", ctok)

        self.layernorm()
        if self.upto == "ln0":
            if self.dbg == "xln":
                self.dump(s.xln, 8, 512)
            self.store_x()
            b.replay()
            return

        self.attention(layer=0)
        if self.upto == "l0a":
            if self.dbg in ("qfm", "kfm", "outh"):
                dsrc = {"qfm": s.q_fm, "kfm": s.k_fm, "outh": s.outh}[self.dbg]
                self.dump(dsrc, 8, 2048)
            self.store_x()
            b.replay()
            return

        self.layernorm()
        if self.moe():
            return
        if self.upto in ("l0g", "l0r"):
            self.store_x()
            b.replay()
            return
        if self.upto == "l0s":
            self.store_x()
            b.replay()
            return
        if self.upto == "l0f":
            if self.dbg == "xlntm":
                self.dump(s.xln_tm, 8, 1024)
            elif self.dbg in (s.dbg_moe or {}):
                tl = [t if not callable(getattr(t, "ap", None)) else t[:] for t in s.dbg_moe[self.dbg]]
                self.dump(tl, len(tl), min(2048, tl[0].ap[-1][1]))
            self.store_x()
            b.replay()
            return

        self.layernorm()
        self.attention(layer=1)
        if self.upto == "l1a":
            self.store_x()
            b.replay()
            return

        self.layernorm()
        self.l1ffn()
        self.store_x()
        b.replay()

    def layernorm(self):
        b, s = self.b, self
        for tt in range(NT):
            col = slice(tt * 512, (tt + 1) * 512)
            ps_m = self.bank(0, parts=1)
            ps_q = self.bank(1, parts=1)
            b.wait("tensor", self._ln_ps_g)
            mm_tok = None
            for i in range(DC):
                st0, st1 = s.xbf_st[i % 2], s.xsq_st[i % 2]
                b.wait("vector", self._ln_g.get(i % 2))
                tv = b.emit("vector", lambda e, st0=st0, i=i, col=col: e.tensor_copy(
                    st0[:], s.x_fm[i][:, col]), inc=s.VS)
                b.wait("scalar", tv)
                tq = b.emit("scalar", lambda e, st0=st0, st1=st1: e.activation(
                    st1[:], st0[:], AF.Square), inc=s.SS)
                b.wait("tensor", tq)
                b.emit("tensor", lambda e, st0=st0, ps_m=ps_m, i=i: e.matmul(
                    ps_m, s.ones[:, 0:1], st0[:], start=(i == 0), stop=(i == DC - 1)))
                mm_tok = b.emit("tensor", lambda e, st1=st1, ps_q=ps_q, i=i: e.matmul(
                    ps_q, s.ones[:, 0:1], st1[:], start=(i == 0), stop=(i == DC - 1)),
                    inc=s.TS)
                self._ln_g[i % 2] = mm_tok
            b.wait("vector", mm_tok)
            b.emit("vector", lambda e, ps_m=ps_m: e.tensor_scalar_mul(ps_m, ps_m, 1.0 / D))
            b.emit("vector", lambda e, ps_q=ps_q: e.tensor_scalar(
                ps_q, ps_q, 1.0 / D, EPS, ALU.mult, ALU.add))
            b.drain("vector")
            tmv = b.emit("vector", lambda e, ps_m=ps_m: e.tensor_copy(s.mrow, ps_m), inc=s.VS)
            b.wait("scalar", tmv)
            cpm = b.emit("scalar", lambda e, ps_m=ps_m: e.copy(s.mr_row[:], ps_m), inc=s.SS)
            b.wait("tensor", cpm)
            lastm = None
            for c4 in range(4):
                pt = self.bank(4 + (c4 % 2))[0:128, 0:128]
                b.wait("tensor", self._mrt_g.get(c4 % 2))
                ttk = b.emit("tensor", lambda e, pt=pt, c4=c4: e.matmul(
                    pt[0:128, 0:1], s.mr_row[:, c4 * 128:(c4 + 1) * 128],
                    s.ones_f[0:1, 0:1], start=True, stop=True), inc=s.TS)
                b.wait("scalar", ttk)
                stok = b.emit("scalar", lambda e, pt=pt, tt=tt, c4=c4: e.copy(
                    s.m_cols[:, tt * 4 + c4: tt * 4 + c4 + 1], pt[0:128, 0:1]), inc=s.SS)
                self._mrt_g[c4 % 2] = stok
                lastm = ttk
            b.wait("scalar", lastm)
            msq = b.emit("scalar", lambda e: e.activation(
                s.mr_row[:], s.mr_row[:], AF.Square), inc=s.SS)
            b.wait("vector", msq)
            tsub = b.emit("vector", lambda e, ps_q=ps_q: e.tensor_sub(
                ps_q, ps_q, s.mr_row[:]), inc=s.VS)
            b.wait("scalar", tsub)
            tq2 = b.emit("scalar", lambda e, ps_q=ps_q: e.activation(
                ps_q, ps_q, AF.Sqrt), inc=s.SS)
            b.wait("vector", tq2)
            b.emit("vector", lambda e, ps_q=ps_q: e.reciprocal(ps_q, ps_q))
            b.drain("vector")
            trr = b.emit("vector", lambda e, ps_q=ps_q: e.tensor_copy(s.rrow, ps_q), inc=s.VS)
            self._ln_ps_g = trr
            b.wait("tensor", tmv)
            b.wait("tensor", trr)
            b.wait("scalar", lastm)
            b.wait("scalar", trr)
            cpk = b.emit("scalar", lambda e, ps_q=ps_q: e.copy(s.mr_row[:], ps_q), inc=s.SS)
            b.wait("tensor", cpk)
            for c4 in range(4):
                pt = self.bank(4 + (c4 % 2))[0:128, 0:128]
                b.wait("tensor", self._mrt_g.get(c4 % 2))
                ttk = b.emit("tensor", lambda e, pt=pt, c4=c4: e.matmul(
                    pt[0:128, 0:1], s.mr_row[:, c4 * 128:(c4 + 1) * 128],
                    s.ones_f[0:1, 0:1], start=True, stop=True), inc=s.TS)
                b.wait("scalar", ttk)
                stok = b.emit("scalar", lambda e, pt=pt, tt=tt, c4=c4: e.copy(
                    s.r_cols[:, tt * 4 + c4: tt * 4 + c4 + 1], pt[0:128, 0:1]), inc=s.SS)
                self._mrt_g[c4 % 2] = stok
            bc_m, bc_r = self.bank(2), self.bank(3)
            b.wait("tensor", self._ln_bc_g)
            b.emit("tensor", lambda e, bc_m=bc_m: e.matmul(
                bc_m, s.ones[0:1, :], s.mrow, start=True, stop=True))
            tb1 = b.emit("tensor", lambda e, bc_r=bc_r: e.matmul(
                bc_r, s.ones[0:1, :], s.rrow, start=True, stop=True), inc=s.TS)
            b.wait("vector", tb1)
            ap_tok = None
            for i in range(DC):
                b.emit("vector", lambda e, i=i, col=col, bc_m=bc_m: e.tensor_sub(
                    s.xln[i][:, col], s.x_fm[i][:, col], bc_m))
                b.drain("vector")
                ap_tok = b.emit("vector", lambda e, i=i, col=col, bc_r=bc_r: e.tensor_mul(
                    s.xln[i][:, col], s.xln[i][:, col], bc_r),
                    inc=(s.VS if i == DC - 1 else None))
                b.drain("vector")
            self._ln_bc_g = ap_tok
        b.join()

    _ln_g = None

    def attention(self, layer):
        b, s = self.b, self
        dram = self.dram
        b.gp_fence()
        rope = (layer == 0)
        wname = "wqkv_s" if rope else "wq1_s"
        bname = "bqkv" if rope else "bqkv1"
        woutname = "wout0" if rope else "wout1"
        bcoln = "bout0_pt" if rope else "bout1_pt"

        if rope:
            b.wait("gpsimd", (s.SCONST, b.semval[s.SCONST]))
            b.emit("gpsimd", lambda e: e.dma_start(
                out=s.cs_cos[:], in_=dram["cs_tab"][:, 0:TC * 32]), inc=s.SCONST, by=16)
            b.wait("gpsimd", (s.SCONST, b.semval[s.SCONST]))
            cs_tok = b.emit("gpsimd", lambda e: e.dma_start(
                out=s.cs_sin[:], in_=dram["cs_tab"][:, TC * 32:]), inc=s.SCONST, by=16)
            b.wait("vector", cs_tok)
        b.wait("gpsimd", (s.SCONST, b.semval[s.SCONST]))
        bc_tok = b.emit("gpsimd", lambda e: e.dma_start(
            out=s.bcol[:], in_=dram[bcoln]), inc=s.SCONST, by=16)

        for c in range(TC):
            vv = s.v_tm[c]
            b.emit("vector", lambda e, vv=vv: e.memset(
                vv.rearrange("p (h u) -> p h u", u=DH + 1)[:, :, DH:DH + 1], 1.0))
        b.drain("vector")

        self.wg_guard = {}
        self.pb_guard = {}
        self.tp_guard = {}
        self.stg_guard = {}
        self._qkv_pend = None
        seqi = 0
        for dt_ in range(12):
            wb = s.wbuf[dt_ % 2]
            bsl = s.bias_sl[dt_ % 2]
            if dt_ >= 2:
                b.wait("gpsimd", self.wg_guard.get(dt_ % 2))
            wt = b.emit("gpsimd", lambda e, wb=wb, dt_=dt_, wname=wname: e.dma_start(
                out=wb[:], in_=dram[wname][dt_]), inc=s.SW[dt_ % 2], by=16)
            bt = b.emit("gpsimd", lambda e, bsl=bsl, dt_=dt_, bname=bname: e.dma_start(
                out=bsl, in_=dram[bname][:, dt_ * 256:(dt_ + 1) * 256]), inc=s.SBIA[dt_ % 2], by=16)
            b.wait("tensor", wt)
            b.wait("tensor", bt)
            for tc in range(TC):
                ctx = self.qkv_tile(layer, dt_, tc, seqi)
                if self._qkv_pend is not None:
                    self.qkv_transpose(*self._qkv_pend)
                    self._qkv_pend = None
                if ctx is not None:
                    self._qkv_pend = ctx
                seqi += 1
        if self._qkv_pend is not None:
            self.qkv_transpose(*self._qkv_pend)
            self._qkv_pend = None
        b.join()

        self.norm_bound(layer)
        b.join()

        wout_toks = []
        b.gp_fence()
        for i in range(DC):
            b.wait("gpsimd", (s.SWOUT, b.semval[s.SWOUT]))
            wout_toks.append(b.emit("gpsimd", lambda e, i=i, woutname=woutname: e.dma_start(
                out=s.wout_sb[i][:], in_=dram[woutname][i]), inc=s.SWOUT, by=16))

        self.stb_guard = {}
        self.pt_guard = {}
        self.pv_guard = {}
        self._rbc_g = None
        pend = None
        for h in range(H):
            pend = self.attn_pair(h, pend)
        self.attn_epilogue(*pend)
        b.join()

        for tok in wout_toks:
            b.wait("tensor", tok)
        b.wait("vector", bc_tok)
        vt_guard = {}
        for tt in range(NT):
            col = slice(tt * 512, (tt + 1) * 512)
            for cc in range(DC):
                bk = (tt * DC + cc) % 2
                pso = self.bank(bk)
                b.wait("tensor", vt_guard.get(bk))
                mt = None
                for ic in range(DC):
                    mt = b.emit("tensor", lambda e, pso=pso, ic=ic, cc=cc, col=col: e.matmul(
                        pso, s.wout_sb[ic][:, ts(cc)], s.outh[ic][:, col],
                        start=(ic == 0), stop=(ic == DC - 1)),
                        inc=(s.TS if ic == DC - 1 else None))
                b.wait("vector", mt)
                vt = b.emit("vector", lambda e, pso=pso, cc=cc, col=col: e.scalar_tensor_tensor(
                    s.x_fm[cc][:, col], pso, s.bcol[:, cc:cc + 1], s.x_fm[cc][:, col],
                    ALU.add, ALU.add), inc=s.VS)
                vt_guard[bk] = vt
        b.join()

    def qkv_tile(self, layer, dt_, tc, seqi):
        b, s = self.b, self
        rope = (layer == 0)
        bk = seqi % 2
        pst = self.PS[0:128, bk * 512:bk * 512 + 256]
        wb = s.wbuf[dt_ % 2]
        bsl = s.bias_sl[dt_ % 2]
        b.wait("tensor", self.pb_guard.get(bk))
        for dc in range(DC):
            b.emit("tensor", lambda e, pst=pst, dc=dc, tc=tc, wb=wb: e.matmul(
                pst, s.xln[dc][:, ts(tc)], wb[:, dc * 256:(dc + 1) * 256],
                start=(dc == 0), stop=False))
        mt = b.emit("tensor", lambda e, pst=pst, tc=tc, bsl=bsl: e.matmul(
            pst, s.ones[0:1, 0:128], bsl, start=False, stop=True), inc=s.TS)
        if tc == TC - 1:
            self.wg_guard[dt_ % 2] = mt

        d0 = dt_ * 256
        if d0 >= 2 * D:
            b.wait("scalar", mt)
            tok = None
            h0 = (d0 - 2 * D) // DH
            for k in range(4):
                tok = b.emit("scalar", lambda e, pst=pst, k=k, h=h0 + k, tc=tc: e.copy(
                    s.v_tm[tc][:, h * (DH + 1):h * (DH + 1) + DH],
                    pst[:, k * DH:(k + 1) * DH]), inc=(s.SS if k == 3 else None))
            self.pb_guard[bk] = tok
            return

        stg = s.stage[seqi % 2]
        b.wait("vector", mt)
        b.wait("vector", self.stg_guard.get(seqi % 2))
        if rope:
            cosb = s.cs_cos[:, tc * 32:(tc + 1) * 32].rearrange(
                "p (a b w) -> p a b w", a=1, b=1).to_broadcast([128, 4, 2, 32])
            sinb = s.cs_sin[:, tc * 32:(tc + 1) * 32].rearrange(
                "p (a b w) -> p a b w", a=1, b=1).to_broadcast([128, 4, 2, 32])
            t2 = self.bank(5)[:, 0:256].rearrange("p (h u w) -> p h u w", u=2, w=32)
            p4 = pst.rearrange("p (h u w) -> p h u w", u=2, w=32)
            sv = stg.rearrange("p (h u w) -> p h u w", u=2, w=32)
            b.emit("vector", lambda e, p4=p4, cosb=cosb, sv=sv: e.tensor_tensor(
                sv, p4, cosb, ALU.mult))
            b.emit("vector", lambda e, p4=p4, sinb=sinb, t2=t2: e.tensor_tensor(
                t2, p4, sinb, ALU.mult))
            b.drain("vector")
            b.emit("vector", lambda e, sv=sv, t2=t2: e.tensor_sub(
                sv[:, :, 0:1, :], sv[:, :, 0:1, :], t2[:, :, 1:2, :]))
            b.drain("vector")
            vt = b.emit("vector", lambda e, sv=sv, t2=t2: e.tensor_add(
                sv[:, :, 1:2, :], sv[:, :, 1:2, :], t2[:, :, 0:1, :]), inc=s.VS)
            b.drain("vector")
        else:
            vt = b.emit("vector", lambda e, stg=stg, pst=pst: e.tensor_copy(
                stg[:], pst), inc=s.VS)
        self.pb_guard[bk] = vt
        isq = d0 < D
        return (stg, seqi, isq, d0 if isq else d0 - D, tc, vt)

    def qkv_transpose(self, stg, seqi, isq, base, tc, vt):
        b, s = self.b, self
        fm = s.q_fm if isq else s.k_fm
        scale = 0.125 if isq else 1.0
        b.wait("tensor", vt)
        for half in range(2):
            tb = 2 + half
            pt = self.bank(tb, dt=BF16)[0:128, 0:128]
            b.wait("tensor", self.tp_guard.get(tb))
            ttk = b.emit("tensor", lambda e, pt=pt, stg=stg, half=half: e.transpose(
                pt, stg[:, half * 128:(half + 1) * 128], s.ident[:]), inc=s.TS)
            if half == 1:
                self.stg_guard[seqi % 2] = ttk
            b.wait("scalar", ttk)
            fi = (base + half * 128) // 128
            stok = b.emit("scalar", lambda e, pt=pt, fm=fm, fi=fi, tc=tc, scale=scale:
                          e.activation(fm[fi][:, ts(tc)], pt, AF.Copy, scale=scale),
                          inc=s.SS)
            self.tp_guard[tb] = stok

    def norm_bound(self, layer):
        b, s = self.b, self
        si = 0
        guard = {}
        rguard = {}
        for tiles in (s.q_fm, s.k_fm):
            for i in range(DC):
                for tt2 in range(NT):
                    col = slice(tt2 * 512, (tt2 + 1) * 512)
                    stg = s.p_t[si % 2][:, 0:512]
                    b.wait("scalar", guard.get(si % 2))
                    tq = b.emit("scalar", lambda e, stg=stg, i=i, col=col, tiles=tiles:
                                e.activation(stg, tiles[i][:, col], AF.Square), inc=s.SS)
                    b.wait("tensor", tq)
                    b.wait("tensor", rguard.get(si % 2))
                    psn = self.bank(4 + si % 2, parts=1)
                    tm = b.emit("tensor", lambda e, stg=stg, psn=psn: e.matmul(
                        psn, s.ones[:, 0:1], stg, start=True, stop=True), inc=s.TS)
                    guard[si % 2] = tm
                    b.wait("vector", tm)
                    rguard[si % 2] = b.emit("vector", lambda e, psn=psn, si=si: e.tensor_reduce(
                        s.slots[:, si:si + 1], psn, AX.X, ALU.max), inc=s.VS)
                    si += 1
        b.drain("vector")
        b.emit("vector", lambda e: e.tensor_reduce(
            s.slots[:, 64:65], s.slots[:, 0:32], AX.X, ALU.max))
        b.emit("vector", lambda e: e.tensor_reduce(
            s.slots[:, 65:66], s.slots[:, 32:64], AX.X, ALU.max))
        b.drain("vector")
        tp = b.emit("vector", lambda e: e.tensor_mul(
            s.slots[:, 66:67], s.slots[:, 64:65], s.slots[:, 65:66]), inc=s.VS)
        b.wait("scalar", tp)
        tsq = b.emit("scalar", lambda e: e.activation(
            s.slots[:, 67:68], s.slots[:, 66:67], AF.Sqrt), inc=s.SS)
        b.wait("vector", tsq)
        b.emit("vector", lambda e: e.tensor_scalar_mul(
            s.mrow[:, 0:1], s.slots[:, 67:68], -1.05))
        trow = b.emit("vector", lambda e: e.drain(), inc=s.VS)
        b.wait("tensor", trow)
        psb = self.bank(6)[0:128, 0:1]
        tb = b.emit("tensor", lambda e, psb=psb: e.matmul(
            psb, s.ones[0:1, :], s.mrow[:, 0:1], start=True, stop=True), inc=s.TS)
        b.wait("scalar", tb)
        b.emit("scalar", lambda e, psb=psb: e.copy(s.mneg_col[:], psb), inc=s.SS)

    def attn_pair(self, h, pend):
        b, s = self.b, self
        off = (h % 2) * 64
        qt = s.q_fm[h // 2]
        kt = s.k_fm[h // 2]
        ptoks = [None, None]
        etoks = [{}, {}]

        def emit_st(qh, kc):
            pss = self.PS[0:128, qh * 1024: qh * 1024 + 1024]
            b.wait("tensor", self.stb_guard.get(qh))
            tokS = None
            for half in range(2):
                tokS = b.emit("tensor", lambda e, pss=pss, kc=kc, half=half,
                              qh=qh: e.matmul(
                    pss[:, half * 512:(half + 1) * 512],
                    kt[off:off + 64, ts(kc)],
                    qt[off:off + 64, qh * 1024 + half * 512:qh * 1024 + (half + 1) * 512],
                    start=True, stop=True), inc=(s.TS if half == 1 else None))
            return tokS

        def emit_exp(kc):
            pe = kc % 2
            b.wait("scalar", self.pt_guard.get(2 * pe))
            b.wait("scalar", self.pt_guard.get(2 * pe + 1))
            etok = b.emit("scalar", lambda e, pe=pe: e.activation(
                s.p_t2[pe][:], self.PS[0:128, 0:2048], AF.Exp,
                bias=s.mneg_col[:]), inc=s.SS)
            self.stb_guard[0] = self.stb_guard[1] = etok
            etoks[0][pe] = etoks[1][pe] = etok

        def emit_pv(qh, kc):
            pvb = 2048 + qh * 1024
            pi = 2 * (kc % 2) + qh
            b.wait("tensor", etoks[qh][kc % 2])
            if kc == 0:
                b.wait("tensor", self.pv_guard.get(qh))
            for half in range(2):
                ptoks[qh] = b.emit("tensor", lambda e, kc=kc, half=half, pi=pi,
                              pvb=pvb: e.matmul(
                    self.PS[0:DH + 1, pvb + half * 512:pvb + (half + 1) * 512],
                    s.v_tm[kc][:, h * (DH + 1):(h + 1) * (DH + 1)],
                    s.p_t[pi][:, half * 512:(half + 1) * 512],
                    start=(kc == 0), stop=(kc == TC - 1)),
                    inc=(s.TS if half == 1 else None))
            self.pt_guard[pi] = ptoks[qh]

        for kc in range(TC):
            emit_st(0, kc)
            tokS = emit_st(1, kc)
            b.wait("scalar", tokS)
            emit_exp(kc)
            if kc == 0 and pend is not None:
                self.attn_epilogue(*pend)
            if kc > 0:
                emit_pv(0, kc - 1)
                emit_pv(1, kc - 1)
        emit_pv(0, TC - 1)
        emit_pv(1, TC - 1)
        return (h, ptoks[0], ptoks[1])

    def attn_epilogue(self, h, ptok0, ptok1):
        b, s = self.b, self
        prev_rbe = None
        for qh, ptok in ((0, ptok0), (1, ptok1)):
            pvb = 2048 + qh * 1024
            b.wait("vector", ptok)
            b.wait("vector", self.stb_guard.get(1))
            b.wait("vector", prev_rbe)
            trc = b.emit("vector", lambda e, pvb=pvb: e.reciprocal(
                self.PS[0:1, 1024:2048], self.PS[DH:DH + 1, pvb:pvb + 1024]), inc=s.VS)
            b.wait("scalar", trc)
            tcp = b.emit("scalar", lambda e: e.copy(
                s.recip_row[:], self.PS[0:1, 1024:2048]), inc=s.SS)
            b.wait("tensor", tcp)
            tb = None
            for half in range(2):
                tb = b.emit("tensor", lambda e, half=half: e.matmul(
                    self.PS[0:64, 1024 + half * 512:1024 + (half + 1) * 512],
                    s.ones[0:1, 0:64], s.recip_row[:, half * 512:(half + 1) * 512],
                    start=True, stop=True), inc=(s.TS if half == 1 else None))
            b.wait("scalar", tb)
            b.wait("scalar", self._rbc_g)
            rbe = b.emit("scalar", lambda e: e.copy(
                s.rbc[0:64, 0:1024], self.PS[0:64, 1024:2048]), inc=s.SS)
            self.stb_guard[1] = rbe
            prev_rbe = rbe
            b.wait("vector", rbe)
            ot = s.outh[h // 2]
            orows = slice((h % 2) * 64, (h % 2) * 64 + 64)
            tm = b.emit("vector", lambda e, ot=ot, orows=orows, qh=qh, pvb=pvb: e.tensor_tensor(
                ot[orows, qh * 1024:(qh + 1) * 1024],
                self.PS[0:64, pvb:pvb + 1024], s.rbc[0:64, 0:1024], ALU.mult), inc=s.VS)
            self.pv_guard[qh] = tm
            self._rbc_g = tm


    def moe(self):
        b, s = self.b, self
        dram = self.dram

        M_XLNTM = 32 * 1024
        M_WS = 65536
        M_HS = 81920
        M_WSD = 114688
        M_SCTM = 118784
        M_WORK = 119808
        M_SCFM = 128000
        M_SELTM = 132096
        M_MARK = 132608
        M_MISC = 133632
        s.xln_tm = [s.av(M_XLNTM + i * 2048, 128, 1024) for i in range(TC)]
        wsA = s.av(M_WS, 128, 4096)
        wsB = s.av(M_WS + 8192, 128, 4096)
        hs = [s.av(M_HS + i * 2048, 128, 1024) for i in range(16)]
        wsd_b = [s.av(M_WSD + i * 2048, 128, 1024) for i in range(2)]
        s.scores_tm = s.av(M_SCTM, 128, 256, F32)
        sel_fm = s.av(M_SCFM, 16, 2048)
        s.scores_work = s.av(M_WORK, 16, 2048, F32)
        sel_tm = s.av(M_SELTM, 128, 256)
        markers = s.av(M_MARK, 128, 256, F32)
        max8 = s.av(M_MISC, 16, 8, F32)
        negmax = s.av(M_MISC + 32, 128, 1, F32)
        den = s.av(M_MISC + 64, 128, 1, F32)
        scst = s.av(M_MISC + 96, 128, 16, F32)
        gate_sb = s.av(M_WSD, 128, 128, F32)
        lt_sb = s.av(M_MISC + 160, 128, 128)
        colsum_bc = s.av(M_MISC + 416, 128, 16, F32)
        s.scores_tm_bf = s.av(M_MISC + 480, 128, 256)

        b.gp_fence()
        b.wait("gpsimd", (s.SCONST, b.semval[s.SCONST]))
        gtok = b.emit("gpsimd", lambda e: e.dma_start(
            out=gate_sb[:], in_=dram["gate_pk"]), inc=s.SCONST, by=16)
        b.wait("gpsimd", (s.SCONST, b.semval[s.SCONST]))
        ltok = b.emit("gpsimd", lambda e: e.dma_start(
            out=lt_sb[:], in_=dram["lt128"]), inc=s.SCONST, by=16)
        b.wait("tensor", gtok)
        b.wait("tensor", ltok)
        pc1 = self.PS[0:1, 3 * 512:3 * 512 + 16]
        mt0 = None
        for dc in range(DC):
            mt0 = b.emit("tensor", lambda e, pc1=pc1, dc=dc: e.matmul(
                pc1, s.ones_f[:, 0:1], gate_sb[:, dc * 16:(dc + 1) * 16],
                start=(dc == 0), stop=(dc == DC - 1)), inc=(s.TS if dc == DC - 1 else None))
        b.wait("scalar", mt0)
        sc0 = b.emit("scalar", lambda e, pc1=pc1: e.copy(scst[0:1, :], pc1), inc=s.SS)
        b.wait("tensor", sc0)
        pc2 = self.PS[0:128, 3 * 512 + 32:3 * 512 + 48]
        mt1 = b.emit("tensor", lambda e, pc2=pc2: e.matmul(
            pc2, s.ones_f[0:1, :], scst[0:1, :], start=True, stop=True), inc=s.TS)
        b.wait("scalar", mt1)
        sc1 = b.emit("scalar", lambda e, pc2=pc2: e.copy(colsum_bc[:], pc2), inc=s.SS)
        b.wait("vector", sc1)

        tp_g = {2: sc1, 3: sc1}
        sc_done = [None] * TC
        for tc in range(TC):
            for dc in range(DC):
                tb = 2 + (dc % 2)
                pt = self.bank(tb, dt=BF16)[0:128, 0:128]
                b.wait("tensor", tp_g.get(tb))
                ttk = b.emit("tensor", lambda e, pt=pt, dc=dc, tc=tc: e.transpose(
                    pt, s.xln[dc][:, ts(tc)], s.ident[:]), inc=s.TS)
                b.wait("scalar", ttk)
                stok = b.emit("scalar", lambda e, pt=pt, dc=dc, tc=tc: e.copy(
                    s.xln_tm[tc][:, ts(dc)], pt), inc=s.SS)
                tp_g[tb] = stok
            bk = tc % 2
            psg = self.PS[0:128, bk * 512:bk * 512 + 16]
            b.wait("tensor", sc_done[tc - 2] if tc >= 2 else None)
            mt = None
            for dc in range(DC):
                mt = b.emit("tensor", lambda e, psg=psg, dc=dc, tc=tc: e.matmul(
                    psg, s.x_fm[dc][:, ts(tc)], gate_sb[:, dc * 16:(dc + 1) * 16],
                    start=(dc == 0), stop=(dc == DC - 1)), inc=(s.TS if dc == DC - 1 else None))
            b.wait("vector", mt)
            b.emit("vector", lambda e, psg=psg, tc=tc: e.scalar_tensor_tensor(
                psg, colsum_bc[:], s.m_cols[:, tc:tc + 1], psg, ALU.mult, ALU.subtract))
            b.drain("vector")
            b.emit("vector", lambda e, psg=psg, tc=tc: e.tensor_scalar(
                psg, psg, s.r_cols[:, tc:tc + 1], -1.0, ALU.mult, ALU.mult))
            b.drain("vector")
            b.wait("vector", self._gate_tex)
            b.emit("vector", lambda e, psg=psg: e.tensor_reduce(
                negmax[:], psg, AX.X, ALU.max, negate=True))
            tvn = b.emit("vector", lambda e: e.drain(), inc=s.VS)
            b.wait("scalar", tvn)
            b.wait("scalar", sc_done[tc - 1] if tc >= 1 else None)
            tex = b.emit("scalar", lambda e, psg=psg: e.activation(
                scst[:], psg, AF.Exp, bias=negmax[:]), inc=s.SS)
            self._gate_tex = tex
            b.wait("vector", tex)
            b.emit("vector", lambda e: e.tensor_reduce(den[:], scst[:], AX.X, ALU.add))
            b.drain("vector")
            b.emit("vector", lambda e: e.reciprocal(den[:], den[:]))
            b.drain("vector")
            b.emit("vector", lambda e, tc=tc: e.tensor_scalar(
                s.scores_tm[:, tc * 16:(tc + 1) * 16], scst[:], den[:], None,
                ALU.mult))
            b.drain("vector")
            sc_done[tc] = b.emit("vector", lambda e, tc=tc: e.tensor_copy(
                s.scores_tm_bf[:, tc * 16:(tc + 1) * 16],
                s.scores_tm[:, tc * 16:(tc + 1) * 16]), inc=s.VS)
        b.join()

        if self.upto == "l0g":
            return
        b.wait("tensor", s._idf_tok)
        tp_g = {}
        for tc in range(TC):
            tb = 2 + (tc % 2)
            pt = self.bank(tb)[0:16, 0:128]
            b.wait("tensor", tp_g.get(tb))
            ttk = b.emit("tensor", lambda e, pt=pt, tc=tc: e.transpose(
                pt, s.scores_tm[:, tc * 16:(tc + 1) * 16], s.ident_f[:]), inc=s.TS)
            b.wait("scalar", ttk)
            stok = b.emit("scalar", lambda e, pt=pt, tc=tc: e.copy(
                s.scores_work[:, ts(tc)], pt), inc=s.SS)
            tp_g[tb] = stok
        b.wait("vector", tp_g[2])
        b.wait("vector", tp_g[3])
        self._sel_left = CAP // 8

        def sel_iter():
            if self._sel_left <= 0:
                return
            self._sel_left -= 1
            b.emit("vector", lambda e: e.max(out=max8[:], in_=s.scores_work[:]))
            b.drain("vector")
            b.emit("vector", lambda e: e.match_replace(
                out=s.scores_work[:], in_to_replace=max8[:], in_values=s.scores_work[:],
                imm_value=-1000.0))
            b.drain("vector")

        b.gp_fence()
        b.wait("tensor", tp_g[2])
        b.wait("tensor", tp_g[3])
        for th in (1, 0):
            tbase = th * 1024
            for q in range(4):
                b.wait("gpsimd", self._shw_g)
                wtokA = b.emit("gpsimd", lambda e, q=q: e.dma_start(
                    out=wsA[:], in_=dram["wsg_s"][q]), inc=s.SA, by=16)
                wtokB = b.emit("gpsimd", lambda e, q=q: e.dma_start(
                    out=wsB[:], in_=dram["wsu_s"][q]), inc=s.SB, by=16)
                b.wait("tensor", wtokA)
                b.wait("tensor", wtokB)
                glast = None
                for fsq in range(4):
                    fs = q * 4 + fsq
                    sel_iter()
                    for tt2 in range(2):
                        col = slice(tbase + tt2 * 512, tbase + (tt2 + 1) * 512)
                        bk = (fs * 2 + tt2) % 2
                        psg = self.PS[0:128, bk * 512: bk * 512 + 512]
                        psu = self.PS[0:128, (2 + bk) * 512: (2 + bk) * 512 + 512]
                        b.wait("tensor", self._sh_g.get(bk))
                        for dc in range(DC):
                            b.emit("tensor", lambda e, psg=psg, fsq=fsq, dc=dc, col=col: e.matmul(
                                psg, wsA[:, (fsq * 8 + dc) * 128:(fsq * 8 + dc + 1) * 128],
                                s.xln[dc][:, col], start=(dc == 0), stop=(dc == DC - 1)))
                            glast = b.emit("tensor", lambda e, psu=psu, fsq=fsq, dc=dc, col=col: e.matmul(
                                psu, wsB[:, (fsq * 8 + dc) * 128:(fsq * 8 + dc + 1) * 128],
                                s.xln[dc][:, col], start=(dc == 0), stop=(dc == DC - 1)),
                                inc=(s.TS if dc == DC - 1 else None))
                        b.wait("scalar", glast)
                        hcol = slice(tt2 * 512, (tt2 + 1) * 512)
                        tgl = b.emit("scalar", lambda e, psg=psg, fs=fs, hcol=hcol: e.activation(
                            hs[fs][:, hcol], psg, AF.Gelu), inc=s.SS)
                        b.wait("vector", tgl)
                        hv = b.emit("vector", lambda e, psu=psu, fs=fs, hcol=hcol: e.tensor_tensor(
                            hs[fs][:, hcol], hs[fs][:, hcol], psu, ALU.mult), inc=s.VS)
                        self._sh_g[bk] = hv
                        self._shw_g = glast
                        if self.upto == "l0w" and fs == 0 and tt2 == 0:
                            b.wait("scalar", hv)
                            b.emit("scalar", lambda e, psg=psg: e.copy(
                                wsd_b[0][:, 0:512], psg), inc=s.SS)
                            b.emit("scalar", lambda e, psu=psu: e.copy(
                                wsd_b[1][:, 0:512], psu), inc=s.SS)
                            self.dump(wsd_b, 2, 512)
                            self.dump([hs[0], wsA, wsB], 3, 4096, pos=1024)
                            self.store_x()
                            b.replay()
                            return True
                        if self.upto == "l0v" and fs == 3 and tt2 == 1:
                            b.wait("gpsimd", hv)
                            self.dump(hs[:4], 4, 1024)
                            self.store_x()
                            b.replay()
                            return True
                        if self.upto == "l0u" and fs == 15 and tt2 == 0:
                            b.wait("scalar", hv)
                            b.emit("scalar", lambda e, psg=psg: e.copy(
                                wsd_b[0][:, 0:512], psg), inc=s.SS)
                            b.emit("scalar", lambda e, psu=psu: e.copy(
                                wsd_b[1][:, 0:512], psu), inc=s.SS)
                            self.dump(wsd_b, 2, 512)
                            self.dump([hs[15]], 1, 1024, pos=1024)
                            self.store_x()
                            b.replay()
                            return True
            for tt2 in range(2):
                col = slice(tbase + tt2 * 512, tbase + (tt2 + 1) * 512)
                hcol = slice(tt2 * 512, (tt2 + 1) * 512)
                dn_tok = [None] * DC
                b.wait("tensor", self._sh_g.get(0))
                b.wait("tensor", self._sh_g.get(1))
                if self.upto == "l0e":
                    self.dump(hs[:4], 4, 1024)
                    self.store_x()
                    b.replay()
                    return True
                for fs in range(16):
                    wq = wsd_b[fs % 2]
                    b.wait("gpsimd", self._wsd_g.get(fs % 2))
                    wt = b.emit("gpsimd", lambda e, fs=fs, wq=wq: e.dma_start(
                        out=wq[:], in_=dram["wsd_s"][fs]), inc=s.SD[fs % 2], by=16)
                    b.wait("tensor", wt)
                    for dc in range(DC):
                        psd = self.bank(dc)
                        dn_tok[dc] = b.emit("tensor", lambda e, psd=psd, wq=wq, fs=fs, dc=dc, hcol=hcol: e.matmul(
                            psd, wq[:, ts(dc)], hs[fs][:, hcol],
                            start=(fs == 0), stop=(fs == 15)),
                            inc=(s.TS if dc == DC - 1 else None))
                    self._wsd_g[fs % 2] = dn_tok[DC - 1]
                b.wait("vector", dn_tok[DC - 1])
                if self.upto == "l0d" and th == 1 and tt2 == 0:
                    b.wait("scalar", dn_tok[DC - 1])
                    for dc in range(DC):
                        b.emit("scalar", lambda e, dc=dc: e.copy(
                            wsA[:, dc * 512:(dc + 1) * 512], self.bank(dc)), inc=s.SS)
                    self.dump([wsB], 1, 4096)
                    self.dump(wsd_b, 2, 1024, pos=4096)
                    self.dump(hs[:4], 4, 1024, pos=4096 + 2048)
                    self.dump(s.xln[:4], 4, 2048, pos=4096 + 2048 + 4096)
                    self.store_x()
                    b.replay()
                    return True
                for dc in range(DC):
                    psd = self.bank(dc)
                    b.emit("vector", lambda e, psd=psd, dc=dc, col=col: e.tensor_add(
                        s.x_fm[dc][:, col], s.x_fm[dc][:, col], psd),
                        inc=(s.VS if dc == DC - 1 else None))
                b.join()
        b.join()

        if self.upto in ("l0s", "l0u", "l0e", "l0v", "l0w"):
            if self.dbg == "hs":
                self.dump(hs, 16, 1024)
            if self.dbg == "wsd":
                self.dump(wsd_b, 2, 1024)
            return
        while self._sel_left > 0:
            sel_iter()
        tsel = b.emit("vector", lambda e: e.tensor_single_scalar(
            sel_fm[:], s.scores_work[:], -999.0, ALU.is_lt), inc=s.VS)
        tp_g = {}
        b.wait("tensor", tsel)
        for tc in range(TC):
            tb = 2 + (tc % 2)
            pt = self.bank(tb, dt=BF16)[0:128, 0:16]
            b.wait("tensor", tp_g.get(tb))
            ttk = b.emit("tensor", lambda e, pt=pt, tc=tc: e.transpose(
                pt, sel_fm[:, ts(tc)], s.ident[0:16, 0:16]), inc=s.TS)
            b.wait("scalar", ttk)
            stok = b.emit("scalar", lambda e, pt=pt, tc=tc: e.copy(
                sel_tm[:, tc * 16:(tc + 1) * 16], pt), inc=s.SS)
            tp_g[tb] = stok
            b.wait("tensor", stok)
        rk_tok = None
        first = True
        for tc in range(TC):
            psr = self.PS[0:128, 4 * 512 + tc * 16: 4 * 512 + (tc + 1) * 16]
            for c2 in range(tc + 1):
                lhs = lt_sb[:] if c2 == tc else s.ones[:]
                last = (tc == TC - 1 and c2 == tc)
                rk_tok = b.emit("tensor", lambda e, psr=psr, c2=c2, lhs=lhs,
                                st=first, sp=last: e.matmul(
                    psr, lhs, sel_tm[:, c2 * 16:(c2 + 1) * 16], start=st, stop=sp),
                    inc=(s.TS if last else None))
                first = False
        b.wait("vector", rk_tok)
        for tc in range(TC):
            psr = self.PS[0:128, 4 * 512 + tc * 16: 4 * 512 + (tc + 1) * 16]
            mk = markers[:, tc * 16:(tc + 1) * 16]
            sl = sel_tm[:, tc * 16:(tc + 1) * 16]
            b.emit("vector", lambda e, psr=psr, mk=mk, sl=sl: e.scalar_tensor_tensor(
                mk, psr, float(CAP), sl, ALU.is_le, ALU.mult))
            b.drain("vector")
            b.emit("vector", lambda e, psr=psr, mk=mk: e.tensor_mul(mk, mk, psr))
            b.drain("vector")
            b.emit("vector", lambda e, mk=mk: e.tensor_scalar_add(mk, mk, -1.0))
            b.drain("vector")
        b.join()
        if self.upto == "l0r":
            return
        M_PERM = 0
        M_XTOK = 8192
        M_HE = 12288
        M_IDX = 20480
        M_PCM = 21504
        M_IOTAT = 81920
        M_IOTA256 = 90112
        M_ICOL = 91136
        M_EO = 92160
        perm = s.av(M_PERM, 128, 4096)
        wsA2 = s.av(119808, 128, 4096)
        wsB2 = s.av(108544, 128, 4096)
        s.wsp = [(wsA, wsB), (wsA2, wsB2)]
        s.wsp_sem = [(s.SA, s.SB), (s.SA2, s.SB2)]
        self._wpair_g = {}
        self._wgrp = 0
        xtok = [s.av(M_XTOK + i * 512, 128, 256) for i in range(DC)]
        h_e = [s.av(M_HE + i * 512, 128, 256) for i in range(16)]
        idx_all = s.av(M_IDX, 128, 8, F32)
        w_sb = s.av(M_IDX + 512, 128, 2, F32)
        pcm = [s.av(M_PCM + i * 1024, 128, 512) for i in range(8)]
        iota_t = s.av(M_IOTAT, 128, 2048, F32)
        iota256 = s.av(M_IOTA256, 128, 256, F32)
        icol2 = s.av(M_ICOL, 128, 32)
        eo_sb = [[s.av(M_EO + (ei * 2 + hf) * 2048, 128, 1024) for hf in range(2)]
                 for ei in range(4)]

        b.gp_fence()
        gi = b.emit("gpsimd", lambda e: e.iota(
            iota_t[:], pattern=[[1, 2048]], base=0, channel_multiplier=0,
            allow_small_or_imprecise_dtypes=True))
        gi = b.emit("gpsimd", lambda e: e.iota(
            iota256[:], pattern=[[1, 256]], base=0, channel_multiplier=0,
            allow_small_or_imprecise_dtypes=True))
        gi = b.emit("gpsimd", lambda e: e.iota(
            icol2.rearrange("p (c two) -> p c two", two=2)[:, :, 0:1],
            pattern=[[0, 16], [0, 1]], base=0, channel_multiplier=1,
            allow_small_or_imprecise_dtypes=True))
        gi = b.emit("gpsimd", lambda e: e.iota(
            icol2.rearrange("p (c two) -> p c two", two=2)[:, :, 1:2],
            pattern=[[128, 16], [0, 1]], base=0, channel_multiplier=0,
            allow_small_or_imprecise_dtypes=True), inc=s.GS)
        b.wait("vector", gi)
        b.wait("tensor", gi)

        for e0 in range(0, self.nexp, 4):
            for ei in range(4):
                ex = e0 + ei
                self.expert(ex, ei, perm, xtok, h_e, idx_all, w_sb, eo_sb,
                            markers, iota256, icol2, wsA, wsB, wsd_b)
            self.scatter4(e0, pcm, iota_t, idx_all, eo_sb)
        b.join()
        s.dbg_moe = {"mcols": [s.m_cols, s.r_cols], "sctm": [s.scores_tm], "xtok": xtok, "h_e": h_e, "perm": [perm],
                     "markers": [markers], "eo": [t for p in eo_sb for t in p],
                     "sworkv": [s.scores_work], "idx": [idx_all]}


    def expert(self, ex, ei, perm, xtok, h_e, idx_all, w_sb, eo_sb,
               markers, iota256, icol2, wsA, wsB, wsd_b):
        b, s = self.b, self
        dram = self.dram
        b.wait("vector", self._perm_g)
        ptok = None
        for tc in range(TC):
            mk = markers[:, tc * 16 + ex: tc * 16 + ex + 1]
            ptok = b.emit("vector", lambda e, tc=tc, mk=mk: e.tensor_scalar(
                perm[:, tc * 256:(tc + 1) * 256], iota256[:], mk, None,
                ALU.is_equal), inc=(s.VS if tc == TC - 1 else None))
        b.wait("tensor", ptok)
        for dc in range(DC):
            bk = dc % 2
            psg = self.PS[0:128, bk * 512: bk * 512 + 256]
            b.wait("tensor", self._g_g.get(bk))
            mt = None
            for tc in range(TC):
                mt = b.emit("tensor", lambda e, psg=psg, tc=tc, dc=dc: e.matmul(
                    psg, s.xln_tm[tc][:, ts(dc)], perm[:, tc * 256:(tc + 1) * 256],
                    start=(tc == 0), stop=(tc == TC - 1)),
                    inc=(s.TS if tc == TC - 1 else None))
            b.wait("scalar", mt)
            st = b.emit("scalar", lambda e, psg=psg, dc=dc: e.copy(
                xtok[dc][:], psg), inc=s.SS)
            self._g_g[bk] = st
        b.wait("tensor", self._g_g.get(0))
        b.wait("tensor", self._g_g.get(1))
        b.wait("tensor", self._up_g.get(0))
        b.wait("tensor", self._up_g.get(1))
        b.wait("vector", self._idx_g)
        for hf in range(2):
            ps_i = self.PS[0:128, hf * 1024: hf * 1024 + 1]
            ps_w = self.PS[0:128, hf * 1024 + 512: hf * 1024 + 513]
            mw = None
            first = True
            for tc in range(TC):
                lh = perm[:, tc * 256 + hf * 128: tc * 256 + (hf + 1) * 128]
                for c2 in range(2):
                    b.emit("tensor", lambda e, ps_i=ps_i, lh=lh, tc=tc, c2=c2,
                           st=first: e.matmul(
                        ps_i, lh, icol2[:, tc * 2 + c2: tc * 2 + c2 + 1],
                        start=st, stop=(tc == TC - 1 and c2 == 1)))
                    first = False
                mw = b.emit("tensor", lambda e, ps_w=ps_w, lh=lh, tc=tc, ex=ex: e.matmul(
                    ps_w, lh, s.scores_tm_bf[:, tc * 16 + ex: tc * 16 + ex + 1],
                    start=(tc == 0), stop=(tc == TC - 1)),
                    inc=(s.TS if tc == TC - 1 else None))
            b.wait("scalar", mw)
            vt = b.emit("scalar", lambda e, ps_i=ps_i, ei=ei, hf=hf: e.copy(
                idx_all[:, ei * 2 + hf: ei * 2 + hf + 1], ps_i), inc=s.SS)
            sw = b.emit("scalar", lambda e, ps_w=ps_w, hf=hf: e.copy(
                w_sb[:, hf:hf + 1], ps_w), inc=s.SS)
            self._w_g[hf] = sw
            self._idxv = vt
            self._g_g[2 * hf] = sw
            self._g_g[2 * hf + 1] = sw
        b.wait("tensor", self._g_g.get(2))
        b.wait("tensor", self._g_g.get(3))
        for q in range(4):
            p = self._wgrp % 2
            self._wgrp += 1
            wA, wB = s.wsp[p]
            semA, semB = s.wsp_sem[p]
            b.wait("gpsimd", self._wpair_g.get(p))
            wtA = b.emit("gpsimd", lambda e, q=q, ex=ex, wA=wA: e.dma_start(
                out=wA[:], in_=dram["wg_s"][ex, q]), inc=semA, by=16)
            wtB = b.emit("gpsimd", lambda e, q=q, ex=ex, wB=wB: e.dma_start(
                out=wB[:], in_=dram["wu_s"][ex, q]), inc=semB, by=16)
            b.wait("tensor", wtA)
            b.wait("tensor", wtB)
            uplast = None
            for fsq in range(4):
                fs = q * 4 + fsq
                psg = self.PS[0:128, (2 + fs % 2) * 512: (2 + fs % 2) * 512 + 256]
                psu = self.PS[0:128, (4 + fs % 2) * 512: (4 + fs % 2) * 512 + 256]
                b.wait("tensor", self._up_g.get(fs % 2))
                for dc in range(DC):
                    b.emit("tensor", lambda e, psg=psg, fsq=fsq, dc=dc, wA=wA: e.matmul(
                        psg, wA[:, (fsq * 8 + dc) * 128:(fsq * 8 + dc + 1) * 128],
                        xtok[dc][:], start=(dc == 0), stop=(dc == DC - 1)))
                    uplast = b.emit("tensor", lambda e, psu=psu, fsq=fsq, dc=dc, wB=wB: e.matmul(
                        psu, wB[:, (fsq * 8 + dc) * 128:(fsq * 8 + dc + 1) * 128],
                        xtok[dc][:], start=(dc == 0), stop=(dc == DC - 1)),
                        inc=(s.TS if dc == DC - 1 else None))
                b.wait("scalar", uplast)
                tgl = b.emit("scalar", lambda e, psg=psg, fs=fs: e.activation(
                    h_e[fs][:], psg, AF.Gelu), inc=s.SS)
                b.wait("vector", tgl)
                hv = b.emit("vector", lambda e, psu=psu, fs=fs: e.tensor_tensor(
                    h_e[fs][:], h_e[fs][:], psu, ALU.mult), inc=s.VS)
                self._up_g[fs % 2] = hv
            self._wpair_g[p] = uplast
        ps_eo = [self.PS[0:128, 2048:3072], self.PS[0:128, 3072:4096]]
        b.wait("tensor", self._eo_g)
        b.wait("tensor", self._up_g.get(0))
        b.wait("tensor", self._up_g.get(1))
        dlast = None
        for q in range(4):
            p = self._wgrp % 2
            self._wgrp += 1
            wq = s.wsp[p][0]
            b.wait("gpsimd", self._wpair_g.get(p))
            wt = b.emit("gpsimd", lambda e, q=q, ex=ex, wq=wq: e.dma_start(
                out=wq[:], in_=dram["wd_s"][ex, q]),
                inc=s.wsp_sem[p][0], by=16)
            b.wait("tensor", wt)
            for fsq in range(4):
                fs = q * 4 + fsq
                b.wait("tensor", self._up_g.get(fs % 2))
                for hf in range(2):
                    for dh in range(2):
                        dlast = b.emit("tensor", lambda e, fs=fs, fsq=fsq, hf=hf, dh=dh, wq=wq: e.matmul(
                            ps_eo[hf][:, dh * 512:(dh + 1) * 512],
                            h_e[fs][:, hf * 128:(hf + 1) * 128],
                            wq[:, fsq * 1024 + dh * 512: fsq * 1024 + (dh + 1) * 512],
                            start=(fs == 0), stop=(fs == 15)),
                            inc=(s.TS if fsq == 3 and hf == 1 and dh == 1 else None))
            self._wpair_g[p] = dlast
        b.wait("scalar", dlast)
        b.wait("scalar", self._eo_scatter_g)
        ev = None
        for hf in range(2):
            b.wait("scalar", self._w_g.get(hf))
            ev = b.emit("scalar", lambda e, hf=hf, ei=ei: e.activation(
                eo_sb[ei][hf][:], ps_eo[hf], AF.Copy, scale=w_sb[:, hf:hf + 1]),
                inc=s.SS)
        self._eo_g = ev
        self._perm_g = dlast

    def scatter4(self, e0, pcm, iota_t, idx_all, eo_sb):
        b, s = self.b, self
        b.wait("tensor", self._eo_g)
        b.wait("vector", self._idxv)
        for tt in range(NT):
            col = slice(tt * 512, (tt + 1) * 512)
            ptoks = []
            for j in range(8):
                b.wait("vector", self._pcm_g.get(j))
                ptoks.append(b.emit("vector", lambda e, j=j, tt=tt: e.tensor_scalar(
                    pcm[j][:], iota_t[:, tt * 512:(tt + 1) * 512],
                    idx_all[:, j:j + 1], None, ALU.is_equal), inc=s.VS))
            for tok in ptoks:
                b.wait("tensor", tok)
            for dc in range(DC):
                bk = (tt * DC + dc) % 2
                pss = self.bank(bk)
                b.wait("tensor", self._sc_g.get(bk))
                mt = None
                for j in range(8):
                    ei, hf = j // 2, j % 2
                    mt = b.emit("tensor", lambda e, pss=pss, ei=ei, hf=hf, j=j, dc=dc: e.matmul(
                        pss, eo_sb[ei][hf][:, ts(dc)], pcm[j][:],
                        start=(j == 0), stop=(j == 7)),
                        inc=(s.TS if j == 7 else None))
                if dc == DC - 1:
                    for j in range(8):
                        self._pcm_g[j] = mt
                b.wait("vector", mt)
                vt = b.emit("vector", lambda e, pss=pss, dc=dc, col=col: e.tensor_add(
                    s.x_fm[dc][:, col], s.x_fm[dc][:, col], pss), inc=s.VS)
                self._sc_g[bk] = vt
        self._eo_scatter_g = mt
        self._idx_g = ptoks[-1]
        for bk in (0, 1):
            if self._sc_g.get(bk) is not None:
                self._g_g[bk] = self._sc_g[bk]


    def l1ffn(self):
        b, s = self.b, self
        dram = self.dram
        F_H = 32 * 1024
        F_WS = 96 * 1024
        F_SCR = 114688
        h_t = [s.av(F_H + j * 2048, 128, 1024) for j in range(HID1 // 128)]
        wsA = s.av(F_WS, 128, 4096)
        wsB = s.av(F_WS + 8192, 128, 4096)
        absb = s.av(F_SCR, 128, 4096)
        tmpf = s.av(F_SCR + 8192, 128, 512, F32)
        arow = s.av(F_SCR + 10240, 1, 128, F32)

        b.gp_fence()
        b.wait("gpsimd", (s.SCONST, b.semval[s.SCONST]))
        bt1 = b.emit("gpsimd", lambda e: e.dma_start(
            out=s.b1col[:], in_=dram["b1_pt"]), inc=s.SCONST, by=16)
        b.wait("gpsimd", (s.SCONST, b.semval[s.SCONST]))
        bt2 = b.emit("gpsimd", lambda e: e.dma_start(
            out=s.b2col[:], in_=dram["b2_pt"]), inc=s.SCONST, by=16)
        b.wait("scalar", bt1)
        b.wait("scalar", bt2)

        wg = {}
        sg = {}
        for ph in range(2):
            tb = ph * 1024
            for j in range(HID1 // 128):
                half = j % 2
                wbuf = s.wbuf[half]
                b.wait("gpsimd", wg.get(half))
                wt = b.emit("gpsimd", lambda e, j=j, wbuf=wbuf: e.dma_start(
                    out=wbuf[:, 0:1024], in_=dram["w1_s"][j]), inc=s.SW[half], by=16)
                b.wait("scalar", wt)
                sdst = wsA if half == 0 else wsB
                b.wait("scalar", sg.get(half))
                if ph == 0:
                    ta = b.emit("scalar", lambda e, wbuf=wbuf: e.activation(
                        absb[:, 0:1024], wbuf[:, 0:1024], AF.Abs), inc=s.SS)
                tsg = b.emit("scalar", lambda e, wbuf=wbuf, sdst=sdst: e.sign(
                    sdst[:, 0:1024], wbuf[:, 0:1024]), inc=s.SS)
                wg[half] = tsg
                b.wait("tensor", tsg)
                for t2 in range(2):
                    col = slice(tb + t2 * 512, tb + (t2 + 1) * 512)
                    bk = (j * 2 + t2) % 2
                    psh = self.bank(bk)
                    b.wait("tensor", self._l1_g.get(bk))
                    mt = None
                    for dc in range(DC):
                        mt = b.emit("tensor", lambda e, psh=psh, sdst=sdst, dc=dc, col=col: e.matmul(
                            psh, sdst[:, dc * 128:(dc + 1) * 128], s.xln[dc][:, col],
                            start=(dc == 0), stop=(dc == DC - 1)),
                            inc=(s.TS if dc == DC - 1 else None))
                    sg[half] = mt
                    if t2 == 0 and ph == 0:
                        b.wait("tensor", ta)
                        b.wait("tensor", self._l1a_g)
                        psa = self.PS[0:1, 1024:1536]
                        tm = None
                        for hh in range(2):
                            tm = b.emit("tensor", lambda e, psa=psa, hh=hh: e.matmul(
                                psa, s.ones[:, 0:1], absb[:, hh * 512:(hh + 1) * 512],
                                start=(hh == 0), stop=(hh == 1)),
                                inc=(s.TS if hh == 1 else None))
                        b.wait("vector", tm)
                        vred = b.emit("vector", lambda e, psa=psa: e.tensor_reduce(
                            self.PS[0:1, 1536:1664],
                            psa.rearrange("p (a c) -> p c a", a=4), AX.X, ALU.add),
                            inc=s.VS)
                        b.wait("scalar", vred)
                        tsc = b.emit("scalar", lambda e: e.activation(
                            arow[:], self.PS[0:1, 1536:1664], AF.Copy,
                            scale=1.0 / D), inc=s.SS)
                        b.wait("tensor", tsc)
                        tcl = b.emit("tensor", lambda e: e.matmul(
                            self.PS[0:128, 1664:1665], arow[:], s.ones_f[0:1, 0:1],
                            start=True, stop=True), inc=s.TS)
                        b.wait("scalar", tcl)
                        ac = b.emit("scalar", lambda e, j=j: e.copy(
                            s.a1col[:, j:j + 1], self.PS[0:128, 1664:1665]), inc=s.SS)
                        b.drain("scalar")
                        self._l1a_g = ac
                    b.wait("scalar", mt)
                    hcol = slice(t2 * 512, (t2 + 1) * 512)
                    ht = b.emit("scalar", lambda e, psh=psh, j=j, hcol=hcol: e.activation(
                        h_t[j][:, hcol], psh, AF.Gelu, scale=s.a1col[:, j:j + 1],
                        bias=s.b1col[:, j:j + 1]), inc=s.SS)
                    self._l1_g[bk] = ht
            for dcb in range(DC):
                half = dcb % 2
                wbuf2 = wsA if half == 0 else wsB
                b.wait("gpsimd", wg.get(2 + half))
                b.wait("gpsimd", sg.get(half))
                wt = b.emit("gpsimd", lambda e, dcb=dcb, wbuf2=wbuf2: e.dma_start(
                    out=wbuf2, in_=dram["w2_s"][dcb]),
                    inc=(s.SA if half == 0 else s.SB), by=16)
                b.wait("scalar", wt)
                if ph == 0:
                    ta = b.emit("scalar", lambda e, wbuf2=wbuf2: e.activation(
                        absb[:], wbuf2[:], AF.Abs), inc=s.SS)
                tsg = b.emit("scalar", lambda e, wbuf2=wbuf2: e.sign(
                    wbuf2[:], wbuf2[:]), inc=s.SS)
                wg[2 + half] = tsg
                b.wait("tensor", tsg)
                for t2 in range(2):
                    col = slice(tb + t2 * 512, tb + (t2 + 1) * 512)
                    hcol = slice(t2 * 512, (t2 + 1) * 512)
                    bk = (dcb * 2 + t2) % 2
                    psd = self.bank(bk)
                    b.wait("tensor", self._l1_g.get(bk))
                    mt = None
                    for j in range(HID1 // 128):
                        mt = b.emit("tensor", lambda e, psd=psd, wbuf2=wbuf2, j=j, hcol=hcol: e.matmul(
                            psd, wbuf2[:, j * 128:(j + 1) * 128], h_t[j][:, hcol],
                            start=(j == 0), stop=(j == HID1 // 128 - 1)),
                            inc=(s.TS if j == HID1 // 128 - 1 else None))
                    sg[half] = mt
                    if t2 == 0 and ph == 0:
                        b.wait("tensor", ta)
                        b.wait("tensor", self._l1a_g)
                        psa = self.PS[0:1, 1024:1536]
                        tm = None
                        for hh in range(8):
                            tm = b.emit("tensor", lambda e, psa=psa, hh=hh: e.matmul(
                                psa, s.ones[:, 0:1], absb[:, hh * 512:(hh + 1) * 512],
                                start=(hh == 0), stop=(hh == 7)),
                                inc=(s.TS if hh == 7 else None))
                        b.wait("vector", tm)
                        vred = b.emit("vector", lambda e, psa=psa: e.tensor_reduce(
                            self.PS[0:1, 1536:1664],
                            psa.rearrange("p (a c) -> p c a", a=4), AX.X, ALU.add),
                            inc=s.VS)
                        b.wait("scalar", vred)
                        tsc = b.emit("scalar", lambda e: e.activation(
                            arow[:], self.PS[0:1, 1536:1664], AF.Copy,
                            scale=1.0 / HID1), inc=s.SS)
                        b.wait("tensor", tsc)
                        tcl = b.emit("tensor", lambda e: e.matmul(
                            self.PS[0:128, 1664:1665], arow[:], s.ones_f[0:1, 0:1],
                            start=True, stop=True), inc=s.TS)
                        b.wait("scalar", tcl)
                        ac = b.emit("scalar", lambda e, dcb=dcb: e.copy(
                            s.a2col[:, dcb:dcb + 1], self.PS[0:128, 1664:1665]), inc=s.SS)
                        self._l1a_g = ac
                    b.wait("vector", self._l1a_g)
                    b.wait("vector", mt)
                    et = b.emit("vector", lambda e, psd=psd, dcb=dcb: e.tensor_scalar(
                        tmpf[:], psd, s.a2col[:, dcb:dcb + 1], s.b2col[:, dcb:dcb + 1],
                        ALU.mult, ALU.add), inc=s.VS)
                    b.drain("vector")
                    vt = b.emit("vector", lambda e, dcb=dcb, col=col: e.tensor_add(
                        s.x_fm[dcb][:, col], s.x_fm[dcb][:, col], tmpf[:]), inc=s.VS)
                    self._l1_g[bk] = vt
                    b.drain("vector")
            b.join()
        b.join()

    def dump(self, tiles, n, cols, pos=0):
        b, s = self.b, self
        tiles = [t[:] if hasattr(t, "alloc_sbuf_tensor") or not hasattr(t, "ap") or callable(getattr(t, "ap")) else t for t in tiles]
        jt = b.join()
        for eng, tok in jt.items():
            b.wait("gpsimd", tok)
        for i in range(min(n, len(tiles))):
            if pos + cols > 16384:
                return
            nparts = tiles[i].ap[0][1]
            b.emit("gpsimd", lambda e, i=i, pos=pos, cols=cols, nparts=nparts: e.dma_start(
                out=self.dbg_t[0:nparts, pos:pos + cols], in_=tiles[i][:, 0:cols]),
                inc=s.STG, by=16)
            pos += cols
        b.wait("gpsimd", (s.STG, b.semval[s.STG]))

    def store_x(self):
        b, s = self.b, self
        jt = b.join()
        for eng, tok in jt.items():
            b.wait("sync", tok)
        outv = self.out.rearrange("(i p) t -> p i t", p=128)
        inv = s.x_all[:].rearrange("p (i t) -> p i t", i=DC)
        b.emit("sync", lambda e: e.dma_start(out=outv, in_=inv), inc=s.ST, by=16)
        b.wait("sync", (s.ST, self.b.semval[s.ST]))


_prog_init = Prog.__init__
def _prog_init2(self, *a, **kw):
    _prog_init(self, *a, **kw)
    self._ln_g = {}
    self._ln_ps_g = None
    self._ln_bc_g = None
    self._sh_g = {}
    self._wsd_g = {}
    self._shw_g = None
    self._mrt_g = {}
    self._gate_tex = None
    self._l1_g = {}
    self._l1a_g = None
    self._idxv = None
    self._perm_g = None
    self._g_g = {}
    self._idx_g = None
    self._w_g = {}
    self._wsAB_g = None
    self._up_g = {}
    self._eo_g = None
    self._eo_scatter_g = None
    self._pcm_g = {}
    self._sc_g = {}
Prog.__init__ = _prog_init2



def _const_arrays():
    t = np.arange(T, dtype=np.float64)[:, None]
    inv = 1.0 / (10000.0 ** (np.arange(0, DH, 2, dtype=np.float64) / DH))
    fr = t * inv[None, :]
    cos = np.cos(fr).astype(np.float32)
    sin = np.sin(fr).astype(np.float32)

    def pack(a):
        return np.ascontiguousarray(
            a.reshape(TC, 128, 32).transpose(1, 0, 2).reshape(128, TC * 32))
    cs = np.concatenate([pack(cos), pack(sin)], axis=1)
    return {"cs_tab": cs, "ident": np.eye(128, dtype=np.float32)}


def _qkv_tiles(w):
    return np.ascontiguousarray(
        w.reshape(DC, 128, 12, 256).transpose(2, 1, 0, 3).reshape(12, 128, 2048))


def prep_inputs(inputs):
    f32 = np.float32
    A = np.ascontiguousarray
    g = dict(_const_arrays())
    g["wqkv_s"] = _qkv_tiles(np.asarray(inputs["L0_attn_W_qkv"], f32))
    g["bqkv"] = A(np.asarray(inputs["L0_attn_b_qkv"], f32).reshape(1, 3 * D))
    g["wout0"] = A(np.asarray(inputs["L0_attn_W_out"], f32).reshape(DC, 128, D))
    g["bout0_pt"] = A(np.asarray(inputs["L0_attn_b_out"], f32).reshape(DC, 128).T)
    gw = np.asarray(inputs["L0_ffn_gate_w"], f32).T
    gp = np.zeros((128, 128), f32)
    gp[:, :DC * E] = gw.reshape(DC, 128, E).transpose(1, 0, 2).reshape(128, DC * E)
    g["gate_pk"] = A(gp)
    g["lt128"] = A(np.triu(np.ones((128, 128), f32)))
    wg = np.asarray(inputs["L0_ffn_Wg"], f32)
    wu = np.asarray(inputs["L0_ffn_Wu"], f32)
    wd = np.asarray(inputs["L0_ffn_Wd"], f32)
    def up_tiles(w):
        wt = w.transpose(0, 2, 1)
        return A(wt.reshape(E, DC, 128, 16, 128)
                 .transpose(0, 3, 2, 1, 4)
                 .reshape(E, 16, 128, DC * 128)
                 .reshape(E, 4, 4, 128, 1024)
                 .transpose(0, 1, 3, 2, 4)
                 .reshape(E, 4, 128, 4096))
    g["wg_s"] = up_tiles(wg)
    g["wu_s"] = up_tiles(wu)
    wdt = wd.transpose(0, 2, 1)
    g["wd_s"] = A(wdt.reshape(E, 4, 4, 128, D).transpose(0, 1, 3, 2, 4)
                  .reshape(E, 4, 128, 4096))
    wsg = np.asarray(inputs["L0_ffn_Ws_g"], f32)
    wsu = np.asarray(inputs["L0_ffn_Ws_u"], f32)
    wsd = np.asarray(inputs["L0_ffn_Ws_d"], f32)
    def up1_tiles(w):
        wt = w.T
        return A(wt.reshape(DC, 128, 16, 128).transpose(2, 1, 0, 3)
                 .reshape(16, 128, DC * 128)
                 .reshape(4, 4, 128, 1024).transpose(0, 2, 1, 3)
                 .reshape(4, 128, 4096))
    g["wsg_s"] = up1_tiles(wsg)
    g["wsu_s"] = up1_tiles(wsu)
    g["wsd_s"] = A(wsd.T.reshape(FSH // 128, 128, D))
    wq = np.asarray(inputs["L1_attn_W_Q"], f32).transpose(2, 0, 1).reshape(D, D)
    wk = np.asarray(inputs["L1_attn_W_K"], f32).transpose(2, 0, 1).reshape(D, D)
    wv = np.asarray(inputs["L1_attn_W_V"], f32).transpose(2, 0, 1).reshape(D, D)
    g["wq1_s"] = _qkv_tiles(np.concatenate([wq, wk, wv], axis=1))
    bq = np.asarray(inputs["L1_attn_b_Q"], f32).reshape(D)
    bk = np.asarray(inputs["L1_attn_b_K"], f32).reshape(D)
    bv = np.asarray(inputs["L1_attn_b_V"], f32).reshape(D)
    g["bqkv1"] = A(np.concatenate([bq, bk, bv]).reshape(1, 3 * D))
    g["wout1"] = A(np.asarray(inputs["L1_attn_W_O"], f32)
                   .transpose(0, 2, 1).reshape(D, D).reshape(DC, 128, D))
    g["bout1_pt"] = A(np.asarray(inputs["L1_attn_b_O"], f32).reshape(DC, 128).T)
    w1 = np.asarray(inputs["L1_ffn_w1"], f32)
    w2 = np.asarray(inputs["L1_ffn_w2"], f32)
    g["w1_s"] = A(w1.reshape(HID1 // 128, 128, DC, 128)
                  .transpose(0, 3, 2, 1).reshape(HID1 // 128, 128, D))
    g["b1_pt"] = A(np.asarray(inputs["L1_ffn_b1"], f32).reshape(HID1 // 128, 128).T)
    g["w2_s"] = A(w2.reshape(DC, 128, HID1 // 128, 128)
                  .transpose(0, 3, 2, 1).reshape(DC, 128, HID1))
    g["b2_pt"] = A(np.asarray(inputs["L1_ffn_b2"], f32).reshape(DC, 128).T)
    xs = [A(np.asarray(inputs["L0_attn_x"][c], f32).T
            .reshape(DC, 128, T).transpose(1, 0, 2).reshape(128, DC * T))
          for c in range(8)]
    return g, xs


def kernel(**inputs):
    g, xs = prep_inputs(inputs)
    nc = Prog().build()
    in_maps = [dict(g, xT=x) for x in xs]
    res = run_bass_kernel_spmd(nc, in_maps, core_ids=list(range(8)))
    out = np.stack([np.asarray(r["outT"]).T for r in res.results])
    return np.ascontiguousarray(out.astype(np.float32))


# version 11 (final)
# speedup vs baseline: 1.0483x; 1.0483x over previous
import numpy as np
from contextlib import ExitStack

import concourse.bass as bass
import concourse.mybir as mybir
from concourse.bass_utils import run_bass_kernel_spmd

F32 = mybir.dt.float32
BF16 = mybir.dt.bfloat16
AF = mybir.ActivationFunctionType
ALU = mybir.AluOpType
AX = mybir.AxisListType

T = 2048
D = 1024
H = 16
DH = 64
E = 16
CAP = 256
FEXP = 2048
FSH = 2048
HID1 = 4096
DC = D // 128
TC = T // 128
NT = T // 512
EPS = 1e-5

ENGINES = ("tensor", "vector", "scalar", "sync", "gpsimd")


class B:

    def __init__(self, nc):
        self.nc = nc
        self.plan = {e: [] for e in ENGINES}
        self.seq = {e: 0 for e in ENGINES}
        self.drain_at = {e: -1 for e in ENGINES}
        self.sems = {}
        self.semval = {}
        self.hw = {}
        self.ctx = ExitStack()

    def semaphore(self, name):
        h = self.ctx.enter_context(self.nc.semaphore(name))
        self.sems[name] = h
        self.semval[name] = 0
        return name

    def emit(self, eng, fn, inc=None, by=1):
        self.seq[eng] += 1
        if inc is not None:
            self.semval[inc] += by
            val = self.semval[inc]
            sem = self.sems[inc]
            self.plan[eng].append(
                lambda e, fn=fn, sem=sem, by=by: fn(e).then_inc(sem, by))
            return (inc, val)
        self.plan[eng].append(fn)
        return None

    def wait(self, eng, tok):
        if tok is None:
            return
        name, val = tok
        if self.hw.get((eng, name), 0) >= val:
            return
        self.hw[(eng, name)] = val
        sem = self.sems[name]
        self.seq[eng] += 1
        self.plan[eng].append(lambda e, sem=sem, val=val: e.wait_ge(sem, val))

    def drain(self, eng):
        if self.drain_at[eng] >= self.seq[eng]:
            return
        self.drain_at[eng] = self.seq[eng] + 1
        self.seq[eng] += 1
        self.plan[eng].append(lambda e: e.drain())

    def join(self, engines=("tensor", "vector", "scalar")):
        toks = {}
        for eng in engines:
            sem = "J" + eng[0].upper()
            if sem not in self.sems:
                self.semaphore(sem)
            toks[eng] = self.emit(eng, lambda e: e.drain(), inc=sem)
            self.drain_at[eng] = self.seq[eng]
        for eng in engines:
            for other, tok in toks.items():
                if other != eng:
                    self.wait(eng, tok)
        self.last_join = toks
        return toks

    def gp_fence(self):
        for tok in getattr(self, "last_join", {}).values():
            self.wait("gpsimd", tok)

    def replay(self):
        nc = self.nc
        with nc.Block() as block:
            @block.tensor
            def _(t):
                for fn in self.plan["tensor"]:
                    fn(t)

            @block.vector
            def _(v):
                for fn in self.plan["vector"]:
                    fn(v)

            @block.scalar
            def _(s):
                for fn in self.plan["scalar"]:
                    fn(s)

            @block.sync
            def _(s):
                for fn in self.plan["sync"]:
                    fn(s)

            @block.gpsimd
            def _(g):
                for fn in self.plan["gpsimd"]:
                    fn(g)


def ts(i, n=128):
    return slice(i * n, (i + 1) * n)


class Prog:
    def __init__(self, upto="all", dbg=None, nexp=E):
        self.upto = upto
        self.dbg = dbg
        self.nexp = nexp
        import os as _os
        self.nc = bass.Bass("TRN2", target_bir_lowering=False,
                            detect_race_conditions=bool(_os.environ.get("RACEDET")))
        self.dram = {}
        self.b = B(self.nc)

    def din(self, name, shape):
        self.dram[name] = self.nc.dram_tensor(
            name, list(shape), F32, kind="ExternalInput").ap()

    def av(self, byte_off, parts, cols, dt=BF16):
        assert byte_off % 32 == 0
        esz = 2 if dt == BF16 else 4
        assert byte_off + cols * esz <= self.arena_bytes, (byte_off, cols, dt)
        full = self.AR[:].bitcast(dt) if dt != F32 else self.AR[:]
        e0 = byte_off // esz
        return full[0:parts, e0:e0 + cols]

    def bank(self, i, parts=128, n=1, dt=F32, p0=0):
        if dt == F32:
            return self.PS[p0:p0 + parts, i * 512:(i + n) * 512]
        full = self.PS[:].bitcast(BF16)
        return full[p0:p0 + parts, i * 1024:(i + n) * 1024]

    def build(self):
        nc, b = self.nc, self.b
        for name, shape in [
            ("xT", (128, DC * T)), ("cs_tab", (128, TC * DH)), ("ident", (128, 128)),
            ("wqkv_s", (12, 128, 2048)), ("bqkv", (1, 3 * D)),
            ("wout0", (DC, 128, D)), ("bout0_pt", (128, DC)),
            ("gate_pk", (128, 128)), ("lt128", (128, 128)),
            ("wg_s", (E, 4, 128, 4096)),
            ("wu_s", (E, 4, 128, 4096)),
            ("wd_s", (E, 4, 128, 4096)),
            ("wsg_s", (4, 128, 4096)),
            ("wsu_s", (4, 128, 4096)),
            ("wsd_s", (FSH // 128, 128, D)),
            ("wq1_s", (12, 128, 2048)), ("bqkv1", (1, 3 * D)),
            ("wout1", (DC, 128, D)), ("bout1_pt", (128, DC)),
            ("w1_s", (HID1 // 128, 128, D)), ("b1_pt", (128, HID1 // 128)),
            ("w2_s", (DC, 128, HID1)), ("b2_pt", (128, DC)),
        ]:
            self.din(name, shape)
        self.out = nc.dram_tensor("outT", [D, T], F32, kind="ExternalOutput").ap()
        self.dbg_t = None
        if self.dbg is not None:
            self.dbg_t = nc.dram_tensor("dbg", [128, 16384], F32,
                                        kind="ExternalOutput").ap()

        ctx = self.b.ctx
        with ctx:
            s = self

            def sbuf(name, shape, dt=BF16):
                return ctx.enter_context(nc.sbuf_tensor(name, list(shape), dt))

            s.x_all = sbuf("x_all", (128, DC * T), F32)
            s.x_fm = [s.x_all[:, i * T:(i + 1) * T] for i in range(DC)]
            s.wbuf = [sbuf(f"wbuf{i}", (128, 2048), BF16) for i in range(2)]
            s.ident = sbuf("id", (128, 128), BF16)
            s.ones = sbuf("one", (128, 128), BF16)
            s.rows = sbuf("rows", (1, 1024), BF16)
            s.mrow = s.rows[:, 0:512]
            s.rrow = s.rows[:, 512:1024]
            s.bias_sl = [s.rows[:, 0:256], s.rows[:, 256:512]]
            s.slots = sbuf("slots", (1, 72), F32)
            s.mneg_col = sbuf("mnegc", (128, 1), F32)
            s.bcol = sbuf("bcol", (128, DC), F32)
            s.b1col = sbuf("b1col", (128, HID1 // 128), F32)
            s.b2col = sbuf("b2col", (128, DC), F32)
            s.a1col = sbuf("a1col", (128, HID1 // 128), F32)
            s.a2col = sbuf("a2col", (128, DC), F32)
            s.m_cols = sbuf("mcols", (128, TC), F32)
            s.r_cols = sbuf("rcols", (128, TC), F32)
            s.ones_f = sbuf("onesf", (128, 128), F32)
            s.ident_f = sbuf("idf", (128, 128), F32)
            rem = nc.sbuf_bytes_remaining - 64
            s.AR = sbuf("arena", (128, rem // 4), F32)
            s.arena_bytes = (rem // 4) * 4
            s.PS = ctx.enter_context(nc.psum_tensor("ps", [128, 4096], F32))

            s.LDX = b.semaphore("LDX")
            s.LDW = b.semaphore("LDW")
            s.ST = b.semaphore("ST")
            s.TS = b.semaphore("TS")
            s.VS = b.semaphore("VS")
            s.SS = b.semaphore("SS")
            s.STG = b.semaphore("STG")
            s.GS = b.semaphore("GS")
            s.SCONST = b.semaphore("SCONST")
            s.SA = b.semaphore("SA")
            s.SB = b.semaphore("SB")
            s.SA2 = b.semaphore("SA2")
            s.SB2 = b.semaphore("SB2")
            s.SD = [b.semaphore("SD0"), b.semaphore("SD1"),
                    b.semaphore("SD2"), b.semaphore("SD3")]
            s.SW = [b.semaphore("SW0"), b.semaphore("SW1")]
            s.SBIA = [b.semaphore("SBIA0"), b.semaphore("SBIA1")]
            s.SWOUT = b.semaphore("SWOUT")

            self.emit_phases()
        return nc

    def emit_phases(self):
        b, s = self.b, self
        dram = self.dram

        A_XLN = 0
        A_K = 32 * 1024
        A_V = 64 * 1024
        A_Q = A_V + TC * (H * (DH + 1)) * 2
        A_CS = A_Q + 32 * 1024
        A_STG = A_CS + 2 * 1024
        assert A_STG + 1024 <= s.arena_bytes, (A_STG, s.arena_bytes)
        s.xln = [s.av(A_XLN + i * 4096, 128, T) for i in range(DC)]
        s.k_fm = [s.av(A_K + i * 4096, 128, T) for i in range(DC)]
        s.v_tm = [s.av(A_V + i * 2 * H * (DH + 1), 128, H * (DH + 1)) for i in range(TC)]
        s.q_fm = [s.av(A_Q + i * 4096, 128, T) for i in range(DC)]
        s.cs_cos = s.av(A_CS, 128, TC * 32)
        s.cs_sin = s.av(A_CS + TC * 32 * 2, 128, TC * 32)
        s.stage = [s.av(A_STG + i * 512, 128, 256) for i in range(2)]
        s.ohx = s.av(0, 128, T)
        s.outh = [s.ohx] + s.q_fm[:DC - 1]
        s.p_t = [s.av(4 * 1024 + i * 2048, 128, 1024) for i in range(4)]
        s.recip_row = s.av(12 * 1024, 1, 1024)
        s.rbc = s.av(30 * 1024, 128, 1024)
        s.wout_sb = [s.av(14 * 1024 + i * 2048, 128, 1024) for i in range(DC)]
        s.xbf_st = [s.av(A_K + i * 1024, 128, 512) for i in range(2)]
        s.xsq_st = [s.av(A_K + 2048 + i * 1024, 128, 512) for i in range(2)]
        s.mr_row = s.av(A_K + 4096, 1, 512, F32)

        xt0 = b.emit("sync", lambda e: e.dma_start(
            out=s.x_all[:], in_=dram["xT"]), inc=s.LDX, by=16)
        x_toks = [xt0]
        b.wait("gpsimd", (s.SCONST, b.semval[s.SCONST]))
        ctok = b.emit("gpsimd", lambda e: e.dma_start(
            out=s.ident[:], in_=dram["ident"]), inc=s.SCONST, by=16)
        b.wait("sync", xt0)
        ctok2 = b.emit("sync", lambda e: e.dma_start(
            out=s.ident_f[:], in_=dram["ident"]), inc=s.LDX, by=16)
        s._idf_tok = ctok2
        b.emit("vector", lambda e: e.memset(s.ones[:], 1.0))
        b.emit("vector", lambda e: e.memset(s.ones_f[:], 1.0))
        b.drain("vector")
        for tok in x_toks:
            b.wait("vector", tok)
            b.wait("tensor", tok)
        b.wait("tensor", ctok)
        b.wait("vector", ctok)

        self.layernorm()
        if self.upto == "ln0":
            if self.dbg == "xln":
                self.dump(s.xln, 8, 512)
            self.store_x()
            b.replay()
            return

        self.attention(layer=0)
        if self.upto == "l0a":
            if self.dbg in ("qfm", "kfm", "outh"):
                dsrc = {"qfm": s.q_fm, "kfm": s.k_fm, "outh": s.outh}[self.dbg]
                self.dump(dsrc, 8, 2048)
            self.store_x()
            b.replay()
            return

        self.layernorm()
        if self.moe():
            return
        if self.upto in ("l0g", "l0r"):
            self.store_x()
            b.replay()
            return
        if self.upto == "l0s":
            self.store_x()
            b.replay()
            return
        if self.upto == "l0f":
            if self.dbg == "xlntm":
                self.dump(s.xln_tm, 8, 1024)
            elif self.dbg in (s.dbg_moe or {}):
                tl = [t if not callable(getattr(t, "ap", None)) else t[:] for t in s.dbg_moe[self.dbg]]
                self.dump(tl, len(tl), min(2048, tl[0].ap[-1][1]))
            self.store_x()
            b.replay()
            return

        self.layernorm()
        self.attention(layer=1)
        if self.upto == "l1a":
            self.store_x()
            b.replay()
            return

        self.layernorm()
        self.l1ffn()
        self.store_x()
        b.replay()

    def layernorm(self):
        b, s = self.b, self
        for tt in range(NT):
            col = slice(tt * 512, (tt + 1) * 512)
            ps_m = self.bank(0, parts=1)
            ps_q = self.bank(1, parts=1)
            b.wait("tensor", self._ln_ps_g)
            mm_tok = None
            for i in range(DC):
                st0, st1 = s.xbf_st[i % 2], s.xsq_st[i % 2]
                b.wait("vector", self._ln_g.get(i % 2))
                tv = b.emit("vector", lambda e, st0=st0, i=i, col=col: e.tensor_copy(
                    st0[:], s.x_fm[i][:, col]), inc=s.VS)
                b.wait("scalar", tv)
                tq = b.emit("scalar", lambda e, st0=st0, st1=st1: e.activation(
                    st1[:], st0[:], AF.Square), inc=s.SS)
                b.wait("tensor", tq)
                b.emit("tensor", lambda e, st0=st0, ps_m=ps_m, i=i: e.matmul(
                    ps_m, s.ones[:, 0:1], st0[:], start=(i == 0), stop=(i == DC - 1)))
                mm_tok = b.emit("tensor", lambda e, st1=st1, ps_q=ps_q, i=i: e.matmul(
                    ps_q, s.ones[:, 0:1], st1[:], start=(i == 0), stop=(i == DC - 1)),
                    inc=s.TS)
                self._ln_g[i % 2] = mm_tok
            b.wait("vector", mm_tok)
            b.emit("vector", lambda e, ps_m=ps_m: e.tensor_scalar_mul(ps_m, ps_m, 1.0 / D))
            b.emit("vector", lambda e, ps_q=ps_q: e.tensor_scalar(
                ps_q, ps_q, 1.0 / D, EPS, ALU.mult, ALU.add))
            b.drain("vector")
            tmv = b.emit("vector", lambda e, ps_m=ps_m: e.tensor_copy(s.mrow, ps_m), inc=s.VS)
            b.wait("scalar", tmv)
            cpm = b.emit("scalar", lambda e, ps_m=ps_m: e.copy(s.mr_row[:], ps_m), inc=s.SS)
            b.wait("tensor", cpm)
            lastm = None
            for c4 in range(4):
                pt = self.bank(4 + (c4 % 2))[0:128, 0:128]
                b.wait("tensor", self._mrt_g.get(c4 % 2))
                ttk = b.emit("tensor", lambda e, pt=pt, c4=c4: e.matmul(
                    pt[0:128, 0:1], s.mr_row[:, c4 * 128:(c4 + 1) * 128],
                    s.ones_f[0:1, 0:1], start=True, stop=True), inc=s.TS)
                b.wait("scalar", ttk)
                stok = b.emit("scalar", lambda e, pt=pt, tt=tt, c4=c4: e.copy(
                    s.m_cols[:, tt * 4 + c4: tt * 4 + c4 + 1], pt[0:128, 0:1]), inc=s.SS)
                self._mrt_g[c4 % 2] = stok
                lastm = ttk
            b.wait("scalar", lastm)
            msq = b.emit("scalar", lambda e: e.activation(
                s.mr_row[:], s.mr_row[:], AF.Square), inc=s.SS)
            b.wait("vector", msq)
            tsub = b.emit("vector", lambda e, ps_q=ps_q: e.tensor_sub(
                ps_q, ps_q, s.mr_row[:]), inc=s.VS)
            b.wait("scalar", tsub)
            tq2 = b.emit("scalar", lambda e, ps_q=ps_q: e.activation(
                ps_q, ps_q, AF.Sqrt), inc=s.SS)
            b.wait("vector", tq2)
            b.emit("vector", lambda e, ps_q=ps_q: e.reciprocal(ps_q, ps_q))
            b.drain("vector")
            trr = b.emit("vector", lambda e, ps_q=ps_q: e.tensor_copy(s.rrow, ps_q), inc=s.VS)
            self._ln_ps_g = trr
            b.wait("tensor", tmv)
            b.wait("tensor", trr)
            b.wait("scalar", lastm)
            b.wait("scalar", trr)
            cpk = b.emit("scalar", lambda e, ps_q=ps_q: e.copy(s.mr_row[:], ps_q), inc=s.SS)
            b.wait("tensor", cpk)
            for c4 in range(4):
                pt = self.bank(4 + (c4 % 2))[0:128, 0:128]
                b.wait("tensor", self._mrt_g.get(c4 % 2))
                ttk = b.emit("tensor", lambda e, pt=pt, c4=c4: e.matmul(
                    pt[0:128, 0:1], s.mr_row[:, c4 * 128:(c4 + 1) * 128],
                    s.ones_f[0:1, 0:1], start=True, stop=True), inc=s.TS)
                b.wait("scalar", ttk)
                stok = b.emit("scalar", lambda e, pt=pt, tt=tt, c4=c4: e.copy(
                    s.r_cols[:, tt * 4 + c4: tt * 4 + c4 + 1], pt[0:128, 0:1]), inc=s.SS)
                self._mrt_g[c4 % 2] = stok
            bc_m, bc_r = self.bank(2), self.bank(3)
            b.wait("tensor", self._ln_bc_g)
            b.emit("tensor", lambda e, bc_m=bc_m: e.matmul(
                bc_m, s.ones[0:1, :], s.mrow, start=True, stop=True))
            tb1 = b.emit("tensor", lambda e, bc_r=bc_r: e.matmul(
                bc_r, s.ones[0:1, :], s.rrow, start=True, stop=True), inc=s.TS)
            b.wait("vector", tb1)
            ap_tok = None
            for i in range(DC):
                b.emit("vector", lambda e, i=i, col=col, bc_m=bc_m: e.tensor_sub(
                    s.xln[i][:, col], s.x_fm[i][:, col], bc_m))
                b.drain("vector")
                ap_tok = b.emit("vector", lambda e, i=i, col=col, bc_r=bc_r: e.tensor_mul(
                    s.xln[i][:, col], s.xln[i][:, col], bc_r),
                    inc=(s.VS if i == DC - 1 else None))
                b.drain("vector")
            self._ln_bc_g = ap_tok
        b.join()

    _ln_g = None

    def attention(self, layer):
        b, s = self.b, self
        dram = self.dram
        b.gp_fence()
        rope = (layer == 0)
        wname = "wqkv_s" if rope else "wq1_s"
        bname = "bqkv" if rope else "bqkv1"
        woutname = "wout0" if rope else "wout1"
        bcoln = "bout0_pt" if rope else "bout1_pt"

        if rope:
            b.wait("gpsimd", (s.SCONST, b.semval[s.SCONST]))
            b.emit("gpsimd", lambda e: e.dma_start(
                out=s.cs_cos[:], in_=dram["cs_tab"][:, 0:TC * 32]), inc=s.SCONST, by=16)
            b.wait("gpsimd", (s.SCONST, b.semval[s.SCONST]))
            cs_tok = b.emit("gpsimd", lambda e: e.dma_start(
                out=s.cs_sin[:], in_=dram["cs_tab"][:, TC * 32:]), inc=s.SCONST, by=16)
            b.wait("vector", cs_tok)
        b.wait("gpsimd", (s.SCONST, b.semval[s.SCONST]))
        bc_tok = b.emit("gpsimd", lambda e: e.dma_start(
            out=s.bcol[:], in_=dram[bcoln]), inc=s.SCONST, by=16)

        for c in range(TC):
            vv = s.v_tm[c]
            b.emit("vector", lambda e, vv=vv: e.memset(
                vv.rearrange("p (h u) -> p h u", u=DH + 1)[:, :, DH:DH + 1], 1.0))
        b.drain("vector")

        self.wg_guard = {}
        self.pb_guard = {}
        self.tp_guard = {}
        self.stg_guard = {}
        self._qkv_pend = None
        seqi = 0
        for dt_ in range(12):
            wb = s.wbuf[dt_ % 2]
            bsl = s.bias_sl[dt_ % 2]
            if dt_ >= 2:
                b.wait("gpsimd", self.wg_guard.get(dt_ % 2))
            wt = b.emit("gpsimd", lambda e, wb=wb, dt_=dt_, wname=wname: e.dma_start(
                out=wb[:], in_=dram[wname][dt_]), inc=s.SW[dt_ % 2], by=16)
            bt = b.emit("gpsimd", lambda e, bsl=bsl, dt_=dt_, bname=bname: e.dma_start(
                out=bsl, in_=dram[bname][:, dt_ * 256:(dt_ + 1) * 256]), inc=s.SBIA[dt_ % 2], by=16)
            b.wait("tensor", wt)
            b.wait("tensor", bt)
            for tc in range(TC):
                ctx = self.qkv_tile(layer, dt_, tc, seqi)
                if self._qkv_pend is not None:
                    self.qkv_transpose(*self._qkv_pend)
                    self._qkv_pend = None
                if ctx is not None:
                    self._qkv_pend = ctx
                seqi += 1
        if self._qkv_pend is not None:
            self.qkv_transpose(*self._qkv_pend)
            self._qkv_pend = None
        b.join()

        self.norm_bound(layer)
        b.join()

        wout_toks = []
        b.gp_fence()
        for i in range(DC):
            b.wait("gpsimd", (s.SWOUT, b.semval[s.SWOUT]))
            wout_toks.append(b.emit("gpsimd", lambda e, i=i, woutname=woutname: e.dma_start(
                out=s.wout_sb[i][:], in_=dram[woutname][i]), inc=s.SWOUT, by=16))

        self.stb_guard = {}
        self.pt_guard = {}
        self.pv_guard = {}
        self._rbc_g = None
        pend = None
        for h in range(H):
            pend = self.attn_pair(h, pend)
        self.attn_epilogue(*pend)
        b.join()

        for tok in wout_toks:
            b.wait("tensor", tok)
        b.wait("vector", bc_tok)
        vt_guard = {}
        for tt in range(NT):
            col = slice(tt * 512, (tt + 1) * 512)
            for cc in range(DC):
                bk = (tt * DC + cc) % 2
                pso = self.bank(bk)
                b.wait("tensor", vt_guard.get(bk))
                mt = None
                for ic in range(DC):
                    mt = b.emit("tensor", lambda e, pso=pso, ic=ic, cc=cc, col=col: e.matmul(
                        pso, s.wout_sb[ic][:, ts(cc)], s.outh[ic][:, col],
                        start=(ic == 0), stop=(ic == DC - 1)),
                        inc=(s.TS if ic == DC - 1 else None))
                b.wait("vector", mt)
                vt = b.emit("vector", lambda e, pso=pso, cc=cc, col=col: e.scalar_tensor_tensor(
                    s.x_fm[cc][:, col], pso, s.bcol[:, cc:cc + 1], s.x_fm[cc][:, col],
                    ALU.add, ALU.add), inc=s.VS)
                vt_guard[bk] = vt
        b.join()

    def qkv_tile(self, layer, dt_, tc, seqi):
        b, s = self.b, self
        rope = (layer == 0)
        bk = seqi % 2
        pst = self.PS[0:128, bk * 512:bk * 512 + 256]
        wb = s.wbuf[dt_ % 2]
        bsl = s.bias_sl[dt_ % 2]
        b.wait("tensor", self.pb_guard.get(bk))
        for dc in range(DC):
            b.emit("tensor", lambda e, pst=pst, dc=dc, tc=tc, wb=wb: e.matmul(
                pst, s.xln[dc][:, ts(tc)], wb[:, dc * 256:(dc + 1) * 256],
                start=(dc == 0), stop=False))
        mt = b.emit("tensor", lambda e, pst=pst, tc=tc, bsl=bsl: e.matmul(
            pst, s.ones[0:1, 0:128], bsl, start=False, stop=True), inc=s.TS)
        if tc == TC - 1:
            self.wg_guard[dt_ % 2] = mt

        d0 = dt_ * 256
        if d0 >= 2 * D:
            b.wait("scalar", mt)
            tok = None
            h0 = (d0 - 2 * D) // DH
            for k in range(4):
                tok = b.emit("scalar", lambda e, pst=pst, k=k, h=h0 + k, tc=tc: e.copy(
                    s.v_tm[tc][:, h * (DH + 1):h * (DH + 1) + DH],
                    pst[:, k * DH:(k + 1) * DH]), inc=(s.SS if k == 3 else None))
            self.pb_guard[bk] = tok
            return

        stg = s.stage[seqi % 2]
        b.wait("vector", mt)
        b.wait("vector", self.stg_guard.get(seqi % 2))
        if rope:
            cosb = s.cs_cos[:, tc * 32:(tc + 1) * 32].rearrange(
                "p (a b w) -> p a b w", a=1, b=1).to_broadcast([128, 4, 2, 32])
            sinb = s.cs_sin[:, tc * 32:(tc + 1) * 32].rearrange(
                "p (a b w) -> p a b w", a=1, b=1).to_broadcast([128, 4, 2, 32])
            t2 = self.bank(5)[:, 0:256].rearrange("p (h u w) -> p h u w", u=2, w=32)
            p4 = pst.rearrange("p (h u w) -> p h u w", u=2, w=32)
            sv = stg.rearrange("p (h u w) -> p h u w", u=2, w=32)
            b.emit("vector", lambda e, p4=p4, cosb=cosb, sv=sv: e.tensor_tensor(
                sv, p4, cosb, ALU.mult))
            b.emit("vector", lambda e, p4=p4, sinb=sinb, t2=t2: e.tensor_tensor(
                t2, p4, sinb, ALU.mult))
            b.drain("vector")
            b.emit("vector", lambda e, sv=sv, t2=t2: e.tensor_sub(
                sv[:, :, 0:1, :], sv[:, :, 0:1, :], t2[:, :, 1:2, :]))
            b.drain("vector")
            vt = b.emit("vector", lambda e, sv=sv, t2=t2: e.tensor_add(
                sv[:, :, 1:2, :], sv[:, :, 1:2, :], t2[:, :, 0:1, :]), inc=s.VS)
            b.drain("vector")
        else:
            vt = b.emit("vector", lambda e, stg=stg, pst=pst: e.tensor_copy(
                stg[:], pst), inc=s.VS)
        self.pb_guard[bk] = vt
        isq = d0 < D
        return (stg, seqi, isq, d0 if isq else d0 - D, tc, vt)

    def qkv_transpose(self, stg, seqi, isq, base, tc, vt):
        b, s = self.b, self
        fm = s.q_fm if isq else s.k_fm
        scale = 0.125 if isq else 1.0
        b.wait("tensor", vt)
        for half in range(2):
            tb = 2 + half
            pt = self.bank(tb, dt=BF16)[0:128, 0:128]
            b.wait("tensor", self.tp_guard.get(tb))
            ttk = b.emit("tensor", lambda e, pt=pt, stg=stg, half=half: e.transpose(
                pt, stg[:, half * 128:(half + 1) * 128], s.ident[:]), inc=s.TS)
            if half == 1:
                self.stg_guard[seqi % 2] = ttk
            b.wait("scalar", ttk)
            fi = (base + half * 128) // 128
            stok = b.emit("scalar", lambda e, pt=pt, fm=fm, fi=fi, tc=tc, scale=scale:
                          e.activation(fm[fi][:, ts(tc)], pt, AF.Copy, scale=scale),
                          inc=s.SS)
            self.tp_guard[tb] = stok

    def norm_bound(self, layer):
        b, s = self.b, self
        si = 0
        guard = {}
        rguard = {}
        for tiles in (s.q_fm, s.k_fm):
            for i in range(DC):
                for tt2 in range(NT):
                    col = slice(tt2 * 512, (tt2 + 1) * 512)
                    stg = s.p_t[si % 2][:, 0:512]
                    b.wait("scalar", guard.get(si % 2))
                    tq = b.emit("scalar", lambda e, stg=stg, i=i, col=col, tiles=tiles:
                                e.activation(stg, tiles[i][:, col], AF.Square), inc=s.SS)
                    b.wait("tensor", tq)
                    b.wait("tensor", rguard.get(si % 2))
                    psn = self.bank(4 + si % 2, parts=1)
                    tm = b.emit("tensor", lambda e, stg=stg, psn=psn: e.matmul(
                        psn, s.ones[:, 0:1], stg, start=True, stop=True), inc=s.TS)
                    guard[si % 2] = tm
                    b.wait("vector", tm)
                    rguard[si % 2] = b.emit("vector", lambda e, psn=psn, si=si: e.tensor_reduce(
                        s.slots[:, si:si + 1], psn, AX.X, ALU.max), inc=s.VS)
                    si += 1
        b.drain("vector")
        b.emit("vector", lambda e: e.tensor_reduce(
            s.slots[:, 64:65], s.slots[:, 0:32], AX.X, ALU.max))
        b.emit("vector", lambda e: e.tensor_reduce(
            s.slots[:, 65:66], s.slots[:, 32:64], AX.X, ALU.max))
        b.drain("vector")
        tp = b.emit("vector", lambda e: e.tensor_mul(
            s.slots[:, 66:67], s.slots[:, 64:65], s.slots[:, 65:66]), inc=s.VS)
        b.wait("scalar", tp)
        tsq = b.emit("scalar", lambda e: e.activation(
            s.slots[:, 67:68], s.slots[:, 66:67], AF.Sqrt), inc=s.SS)
        b.wait("vector", tsq)
        b.emit("vector", lambda e: e.tensor_scalar_mul(
            s.mrow[:, 0:1], s.slots[:, 67:68], -1.05))
        trow = b.emit("vector", lambda e: e.drain(), inc=s.VS)
        b.wait("tensor", trow)
        psb = self.bank(6)[0:128, 0:1]
        tb = b.emit("tensor", lambda e, psb=psb: e.matmul(
            psb, s.ones[0:1, :], s.mrow[:, 0:1], start=True, stop=True), inc=s.TS)
        b.wait("scalar", tb)
        b.emit("scalar", lambda e, psb=psb: e.copy(s.mneg_col[:], psb), inc=s.SS)

    def attn_pair(self, h, pend):
        b, s = self.b, self
        off = (h % 2) * 64
        qt = s.q_fm[h // 2]
        kt = s.k_fm[h // 2]
        ptoks = [None, None]
        etoks = [{}, {}]

        def emit_st(qh, kc):
            pss = self.PS[0:128, qh * 1024: qh * 1024 + 1024]
            b.wait("tensor", self.stb_guard.get(qh))
            tokS = None
            for half in range(2):
                tokS = b.emit("tensor", lambda e, pss=pss, kc=kc, half=half,
                              qh=qh: e.matmul(
                    pss[:, half * 512:(half + 1) * 512],
                    kt[off:off + 64, ts(kc)],
                    qt[off:off + 64, qh * 1024 + half * 512:qh * 1024 + (half + 1) * 512],
                    start=True, stop=True), inc=(s.TS if half == 1 else None))
            pi = qh * 2 + kc % 2
            b.wait("scalar", tokS)
            b.wait("scalar", self.pt_guard.get(pi))
            etok = b.emit("scalar", lambda e, pss=pss, pi=pi: e.activation(
                s.p_t[pi][:], pss, AF.Exp, bias=s.mneg_col[:]), inc=s.SS)
            self.stb_guard[qh] = etok
            etoks[qh][kc % 2] = etok

        def emit_pv(qh, kc):
            pvb = 2048 + qh * 1024
            pi = qh * 2 + kc % 2
            b.wait("tensor", etoks[qh][kc % 2])
            if kc == 0:
                b.wait("tensor", self.pv_guard.get(qh))
            for half in range(2):
                ptoks[qh] = b.emit("tensor", lambda e, kc=kc, half=half, pi=pi,
                              pvb=pvb: e.matmul(
                    self.PS[0:DH + 1, pvb + half * 512:pvb + (half + 1) * 512],
                    s.v_tm[kc][:, h * (DH + 1):(h + 1) * (DH + 1)],
                    s.p_t[pi][:, half * 512:(half + 1) * 512],
                    start=(kc == 0), stop=(kc == TC - 1)),
                    inc=(s.TS if half == 1 else None))
            self.pt_guard[pi] = ptoks[qh]

        for kc in range(TC):
            emit_st(0, kc)
            emit_st(1, kc)
            if kc == 0 and pend is not None:
                self.attn_epilogue(*pend)
            if kc > 0:
                emit_pv(0, kc - 1)
                emit_pv(1, kc - 1)
        emit_pv(0, TC - 1)
        emit_pv(1, TC - 1)
        return (h, ptoks[0], ptoks[1])

    def attn_epilogue(self, h, ptok0, ptok1):
        b, s = self.b, self
        prev_rbe = None
        for qh, ptok in ((0, ptok0), (1, ptok1)):
            pvb = 2048 + qh * 1024
            b.wait("vector", ptok)
            b.wait("vector", self.stb_guard.get(1))
            b.wait("vector", prev_rbe)
            trc = b.emit("vector", lambda e, pvb=pvb: e.reciprocal(
                self.PS[0:1, 1024:2048], self.PS[DH:DH + 1, pvb:pvb + 1024]), inc=s.VS)
            b.wait("scalar", trc)
            tcp = b.emit("scalar", lambda e: e.copy(
                s.recip_row[:], self.PS[0:1, 1024:2048]), inc=s.SS)
            b.wait("tensor", tcp)
            tb = None
            for half in range(2):
                tb = b.emit("tensor", lambda e, half=half: e.matmul(
                    self.PS[0:64, 1024 + half * 512:1024 + (half + 1) * 512],
                    s.ones[0:1, 0:64], s.recip_row[:, half * 512:(half + 1) * 512],
                    start=True, stop=True), inc=(s.TS if half == 1 else None))
            b.wait("scalar", tb)
            b.wait("scalar", self._rbc_g)
            rbe = b.emit("scalar", lambda e: e.copy(
                s.rbc[0:64, 0:1024], self.PS[0:64, 1024:2048]), inc=s.SS)
            self.stb_guard[1] = rbe
            prev_rbe = rbe
            b.wait("vector", rbe)
            ot = s.outh[h // 2]
            orows = slice((h % 2) * 64, (h % 2) * 64 + 64)
            tm = b.emit("vector", lambda e, ot=ot, orows=orows, qh=qh, pvb=pvb: e.tensor_tensor(
                ot[orows, qh * 1024:(qh + 1) * 1024],
                self.PS[0:64, pvb:pvb + 1024], s.rbc[0:64, 0:1024], ALU.mult), inc=s.VS)
            self.pv_guard[qh] = tm
            self._rbc_g = tm


    def moe(self):
        b, s = self.b, self
        dram = self.dram

        M_XLNTM = 32 * 1024
        M_WS = 65536
        M_HS = 81920
        M_WSD = 114688
        M_SCTM = 118784
        M_WORK = 119808
        M_SCFM = 128000
        M_SELTM = 132096
        M_MARK = 132608
        M_MISC = 133632
        s.xln_tm = [s.av(M_XLNTM + i * 2048, 128, 1024) for i in range(TC)]
        wsA = s.av(M_WS, 128, 4096)
        wsB = s.av(M_WS + 8192, 128, 4096)
        hs = [s.av(M_HS + i * 2048, 128, 1024) for i in range(16)]
        wsd_b = [s.av(M_WSD, 128, 1024), s.av(M_WSD + 2048, 128, 1024),
                 s.av(M_SCFM, 128, 1024), s.av(M_SCFM + 2048, 128, 1024)]
        s.scores_tm = s.av(M_SCTM, 128, 256, F32)
        sel_fm = s.av(M_SCFM, 16, 2048)
        s.scores_work = s.av(M_WORK, 16, 2048, F32)
        sel_tm = s.av(M_SELTM, 128, 256)
        markers = s.av(M_MARK, 128, 256, F32)
        max8 = s.av(M_MISC, 16, 8, F32)
        negmax = s.av(M_MISC + 32, 128, 1, F32)
        den = s.av(M_MISC + 64, 128, 1, F32)
        scst = s.av(M_MISC + 96, 128, 16, F32)
        gate_sb = s.av(M_WSD, 128, 128, F32)
        lt_sb = s.av(M_MISC + 160, 128, 128)
        colsum_bc = s.av(M_MISC + 416, 128, 16, F32)
        s.scores_tm_bf = s.av(M_MISC + 480, 128, 256)

        b.gp_fence()
        b.wait("gpsimd", (s.SCONST, b.semval[s.SCONST]))
        gtok = b.emit("gpsimd", lambda e: e.dma_start(
            out=gate_sb[:], in_=dram["gate_pk"]), inc=s.SCONST, by=16)
        b.wait("gpsimd", (s.SCONST, b.semval[s.SCONST]))
        ltok = b.emit("gpsimd", lambda e: e.dma_start(
            out=lt_sb[:], in_=dram["lt128"]), inc=s.SCONST, by=16)
        b.wait("tensor", gtok)
        b.wait("tensor", ltok)
        pc1 = self.PS[0:1, 3 * 512:3 * 512 + 16]
        mt0 = None
        for dc in range(DC):
            mt0 = b.emit("tensor", lambda e, pc1=pc1, dc=dc: e.matmul(
                pc1, s.ones_f[:, 0:1], gate_sb[:, dc * 16:(dc + 1) * 16],
                start=(dc == 0), stop=(dc == DC - 1)), inc=(s.TS if dc == DC - 1 else None))
        b.wait("scalar", mt0)
        sc0 = b.emit("scalar", lambda e, pc1=pc1: e.copy(scst[0:1, :], pc1), inc=s.SS)
        b.wait("tensor", sc0)
        pc2 = self.PS[0:128, 3 * 512 + 32:3 * 512 + 48]
        mt1 = b.emit("tensor", lambda e, pc2=pc2: e.matmul(
            pc2, s.ones_f[0:1, :], scst[0:1, :], start=True, stop=True), inc=s.TS)
        b.wait("scalar", mt1)
        sc1 = b.emit("scalar", lambda e, pc2=pc2: e.copy(colsum_bc[:], pc2), inc=s.SS)
        b.wait("vector", sc1)

        tp_g = {2: sc1, 3: sc1}
        sc_done = [None] * TC
        for tc in range(TC):
            for dc in range(DC):
                tb = 2 + (dc % 2)
                pt = self.bank(tb, dt=BF16)[0:128, 0:128]
                b.wait("tensor", tp_g.get(tb))
                ttk = b.emit("tensor", lambda e, pt=pt, dc=dc, tc=tc: e.transpose(
                    pt, s.xln[dc][:, ts(tc)], s.ident[:]), inc=s.TS)
                b.wait("scalar", ttk)
                stok = b.emit("scalar", lambda e, pt=pt, dc=dc, tc=tc: e.copy(
                    s.xln_tm[tc][:, ts(dc)], pt), inc=s.SS)
                tp_g[tb] = stok
            bk = tc % 2
            psg = self.PS[0:128, bk * 512:bk * 512 + 16]
            b.wait("tensor", sc_done[tc - 2] if tc >= 2 else None)
            mt = None
            for dc in range(DC):
                mt = b.emit("tensor", lambda e, psg=psg, dc=dc, tc=tc: e.matmul(
                    psg, s.x_fm[dc][:, ts(tc)], gate_sb[:, dc * 16:(dc + 1) * 16],
                    start=(dc == 0), stop=(dc == DC - 1)), inc=(s.TS if dc == DC - 1 else None))
            b.wait("vector", mt)
            b.emit("vector", lambda e, psg=psg, tc=tc: e.scalar_tensor_tensor(
                psg, colsum_bc[:], s.m_cols[:, tc:tc + 1], psg, ALU.mult, ALU.subtract))
            b.drain("vector")
            b.emit("vector", lambda e, psg=psg, tc=tc: e.tensor_scalar(
                psg, psg, s.r_cols[:, tc:tc + 1], -1.0, ALU.mult, ALU.mult))
            b.drain("vector")
            b.wait("vector", self._gate_tex)
            b.emit("vector", lambda e, psg=psg: e.tensor_reduce(
                negmax[:], psg, AX.X, ALU.max, negate=True))
            tvn = b.emit("vector", lambda e: e.drain(), inc=s.VS)
            b.wait("scalar", tvn)
            b.wait("scalar", sc_done[tc - 1] if tc >= 1 else None)
            tex = b.emit("scalar", lambda e, psg=psg: e.activation(
                scst[:], psg, AF.Exp, bias=negmax[:]), inc=s.SS)
            self._gate_tex = tex
            b.wait("vector", tex)
            b.emit("vector", lambda e: e.tensor_reduce(den[:], scst[:], AX.X, ALU.add))
            b.drain("vector")
            b.emit("vector", lambda e: e.reciprocal(den[:], den[:]))
            b.drain("vector")
            b.emit("vector", lambda e, tc=tc: e.tensor_scalar(
                s.scores_tm[:, tc * 16:(tc + 1) * 16], scst[:], den[:], None,
                ALU.mult))
            b.drain("vector")
            sc_done[tc] = b.emit("vector", lambda e, tc=tc: e.tensor_copy(
                s.scores_tm_bf[:, tc * 16:(tc + 1) * 16],
                s.scores_tm[:, tc * 16:(tc + 1) * 16]), inc=s.VS)
        b.join()

        if self.upto == "l0g":
            return
        b.wait("tensor", s._idf_tok)
        tp_g = {}
        for tc in range(TC):
            tb = 2 + (tc % 2)
            pt = self.bank(tb)[0:16, 0:128]
            b.wait("tensor", tp_g.get(tb))
            ttk = b.emit("tensor", lambda e, pt=pt, tc=tc: e.transpose(
                pt, s.scores_tm[:, tc * 16:(tc + 1) * 16], s.ident_f[:]), inc=s.TS)
            b.wait("scalar", ttk)
            stok = b.emit("scalar", lambda e, pt=pt, tc=tc: e.copy(
                s.scores_work[:, ts(tc)], pt), inc=s.SS)
            tp_g[tb] = stok
        b.wait("vector", tp_g[2])
        b.wait("vector", tp_g[3])
        self._sel_left = CAP // 8

        def sel_iter():
            if self._sel_left <= 0:
                return
            self._sel_left -= 1
            b.emit("vector", lambda e: e.max(out=max8[:], in_=s.scores_work[:]))
            b.drain("vector")
            b.emit("vector", lambda e: e.match_replace(
                out=s.scores_work[:], in_to_replace=max8[:], in_values=s.scores_work[:],
                imm_value=-1000.0))
            b.drain("vector")

        b.gp_fence()
        b.wait("tensor", tp_g[2])
        b.wait("tensor", tp_g[3])
        for th in (1, 0):
            tbase = th * 1024
            for q in range(4):
                b.wait("gpsimd", self._shw_g)
                wtokA = b.emit("gpsimd", lambda e, q=q: e.dma_start(
                    out=wsA[:], in_=dram["wsg_s"][q]), inc=s.SA, by=16)
                wtokB = b.emit("gpsimd", lambda e, q=q: e.dma_start(
                    out=wsB[:], in_=dram["wsu_s"][q]), inc=s.SB, by=16)
                b.wait("tensor", wtokA)
                b.wait("tensor", wtokB)
                glast = None
                for fsq in range(4):
                    fs = q * 4 + fsq
                    sel_iter()
                    for tt2 in range(2):
                        col = slice(tbase + tt2 * 512, tbase + (tt2 + 1) * 512)
                        bk = (fs * 2 + tt2) % 2
                        psg = self.PS[0:128, bk * 512: bk * 512 + 512]
                        psu = self.PS[0:128, (2 + bk) * 512: (2 + bk) * 512 + 512]
                        b.wait("tensor", self._sh_g.get(bk))
                        for dc in range(DC):
                            b.emit("tensor", lambda e, psg=psg, fsq=fsq, dc=dc, col=col: e.matmul(
                                psg, wsA[:, (fsq * 8 + dc) * 128:(fsq * 8 + dc + 1) * 128],
                                s.xln[dc][:, col], start=(dc == 0), stop=(dc == DC - 1)))
                            glast = b.emit("tensor", lambda e, psu=psu, fsq=fsq, dc=dc, col=col: e.matmul(
                                psu, wsB[:, (fsq * 8 + dc) * 128:(fsq * 8 + dc + 1) * 128],
                                s.xln[dc][:, col], start=(dc == 0), stop=(dc == DC - 1)),
                                inc=(s.TS if dc == DC - 1 else None))
                        b.wait("scalar", glast)
                        hcol = slice(tt2 * 512, (tt2 + 1) * 512)
                        tgl = b.emit("scalar", lambda e, psg=psg, fs=fs, hcol=hcol: e.activation(
                            hs[fs][:, hcol], psg, AF.Gelu), inc=s.SS)
                        b.wait("vector", tgl)
                        hv = b.emit("vector", lambda e, psu=psu, fs=fs, hcol=hcol: e.tensor_tensor(
                            hs[fs][:, hcol], hs[fs][:, hcol], psu, ALU.mult), inc=s.VS)
                        self._sh_g[bk] = hv
                        self._shw_g = glast
                        if self.upto == "l0w" and fs == 0 and tt2 == 0:
                            b.wait("scalar", hv)
                            b.emit("scalar", lambda e, psg=psg: e.copy(
                                wsd_b[0][:, 0:512], psg), inc=s.SS)
                            b.emit("scalar", lambda e, psu=psu: e.copy(
                                wsd_b[1][:, 0:512], psu), inc=s.SS)
                            self.dump(wsd_b, 2, 512)
                            self.dump([hs[0], wsA, wsB], 3, 4096, pos=1024)
                            self.store_x()
                            b.replay()
                            return True
                        if self.upto == "l0v" and fs == 3 and tt2 == 1:
                            b.wait("gpsimd", hv)
                            self.dump(hs[:4], 4, 1024)
                            self.store_x()
                            b.replay()
                            return True
                        if self.upto == "l0u" and fs == 15 and tt2 == 0:
                            b.wait("scalar", hv)
                            b.emit("scalar", lambda e, psg=psg: e.copy(
                                wsd_b[0][:, 0:512], psg), inc=s.SS)
                            b.emit("scalar", lambda e, psu=psu: e.copy(
                                wsd_b[1][:, 0:512], psu), inc=s.SS)
                            self.dump(wsd_b, 2, 512)
                            self.dump([hs[15]], 1, 1024, pos=1024)
                            self.store_x()
                            b.replay()
                            return True
            for tt2 in range(2):
                col = slice(tbase + tt2 * 512, tbase + (tt2 + 1) * 512)
                hcol = slice(tt2 * 512, (tt2 + 1) * 512)
                dn_tok = [None] * DC
                b.wait("tensor", self._sh_g.get(0))
                b.wait("tensor", self._sh_g.get(1))
                if self.upto == "l0e":
                    self.dump(hs[:4], 4, 1024)
                    self.store_x()
                    b.replay()
                    return True
                for fs in range(16):
                    wq = wsd_b[fs % 4]
                    b.wait("gpsimd", self._wsd_g.get(fs % 4))
                    wt = b.emit("gpsimd", lambda e, fs=fs, wq=wq: e.dma_start(
                        out=wq[:], in_=dram["wsd_s"][fs]), inc=s.SD[fs % 4], by=16)
                    b.wait("tensor", wt)
                    for dc in range(DC):
                        psd = self.bank(dc)
                        dn_tok[dc] = b.emit("tensor", lambda e, psd=psd, wq=wq, fs=fs, dc=dc, hcol=hcol: e.matmul(
                            psd, wq[:, ts(dc)], hs[fs][:, hcol],
                            start=(fs == 0), stop=(fs == 15)),
                            inc=(s.TS if dc == DC - 1 else None))
                    self._wsd_g[fs % 4] = dn_tok[DC - 1]
                b.wait("vector", dn_tok[DC - 1])
                if self.upto == "l0d" and th == 1 and tt2 == 0:
                    b.wait("scalar", dn_tok[DC - 1])
                    for dc in range(DC):
                        b.emit("scalar", lambda e, dc=dc: e.copy(
                            wsA[:, dc * 512:(dc + 1) * 512], self.bank(dc)), inc=s.SS)
                    self.dump([wsB], 1, 4096)
                    self.dump(wsd_b, 2, 1024, pos=4096)
                    self.dump(hs[:4], 4, 1024, pos=4096 + 2048)
                    self.dump(s.xln[:4], 4, 2048, pos=4096 + 2048 + 4096)
                    self.store_x()
                    b.replay()
                    return True
                for dc in range(DC):
                    psd = self.bank(dc)
                    b.emit("vector", lambda e, psd=psd, dc=dc, col=col: e.tensor_add(
                        s.x_fm[dc][:, col], s.x_fm[dc][:, col], psd),
                        inc=(s.VS if dc == DC - 1 else None))
                b.join()
        b.join()

        if self.upto in ("l0s", "l0u", "l0e", "l0v", "l0w"):
            if self.dbg == "hs":
                self.dump(hs, 16, 1024)
            if self.dbg == "wsd":
                self.dump(wsd_b, 2, 1024)
            return
        while self._sel_left > 0:
            sel_iter()
        tsel = b.emit("vector", lambda e: e.tensor_single_scalar(
            sel_fm[:], s.scores_work[:], -999.0, ALU.is_lt), inc=s.VS)
        tp_g = {}
        b.wait("tensor", tsel)
        for tc in range(TC):
            tb = 2 + (tc % 2)
            pt = self.bank(tb, dt=BF16)[0:128, 0:16]
            b.wait("tensor", tp_g.get(tb))
            ttk = b.emit("tensor", lambda e, pt=pt, tc=tc: e.transpose(
                pt, sel_fm[:, ts(tc)], s.ident[0:16, 0:16]), inc=s.TS)
            b.wait("scalar", ttk)
            stok = b.emit("scalar", lambda e, pt=pt, tc=tc: e.copy(
                sel_tm[:, tc * 16:(tc + 1) * 16], pt), inc=s.SS)
            tp_g[tb] = stok
            b.wait("tensor", stok)
        rk_tok = None
        first = True
        for tc in range(TC):
            psr = self.PS[0:128, 4 * 512 + tc * 16: 4 * 512 + (tc + 1) * 16]
            for c2 in range(tc + 1):
                lhs = lt_sb[:] if c2 == tc else s.ones[:]
                last = (tc == TC - 1 and c2 == tc)
                rk_tok = b.emit("tensor", lambda e, psr=psr, c2=c2, lhs=lhs,
                                st=first, sp=last: e.matmul(
                    psr, lhs, sel_tm[:, c2 * 16:(c2 + 1) * 16], start=st, stop=sp),
                    inc=(s.TS if last else None))
                first = False
        b.wait("vector", rk_tok)
        for tc in range(TC):
            psr = self.PS[0:128, 4 * 512 + tc * 16: 4 * 512 + (tc + 1) * 16]
            mk = markers[:, tc * 16:(tc + 1) * 16]
            sl = sel_tm[:, tc * 16:(tc + 1) * 16]
            b.emit("vector", lambda e, psr=psr, mk=mk, sl=sl: e.scalar_tensor_tensor(
                mk, psr, float(CAP), sl, ALU.is_le, ALU.mult))
            b.drain("vector")
            b.emit("vector", lambda e, psr=psr, mk=mk: e.tensor_mul(mk, mk, psr))
            b.drain("vector")
            b.emit("vector", lambda e, mk=mk: e.tensor_scalar_add(mk, mk, -1.0))
            b.drain("vector")
        b.join()
        if self.upto == "l0r":
            return
        M_PERM = 0
        M_XTOK = 8192
        M_HE = 12288
        M_IDX = 20480
        M_PCM = 21504
        M_IOTAT = 81920
        M_IOTA256 = 90112
        M_ICOL = 91136
        M_EO = 92160
        perm = s.av(M_PERM, 128, 4096)
        wsA2 = s.av(119808, 128, 4096)
        wsB2 = s.av(108544, 128, 4096)
        s.wsp = [(wsA, wsB), (wsA2, wsB2)]
        s.wsp_sem = [(s.SA, s.SB), (s.SA2, s.SB2)]
        self._wpair_g = {}
        self._wgrp = 0
        xtok = [s.av(M_XTOK + i * 512, 128, 256) for i in range(DC)]
        h_e = [s.av(M_HE + i * 512, 128, 256) for i in range(16)]
        idx_all = s.av(M_IDX, 128, 8, F32)
        w_sb = s.av(M_IDX + 512, 128, 2, F32)
        pcm = [s.av(M_PCM + i * 1024, 128, 512) for i in range(8)]
        iota_t = s.av(M_IOTAT, 128, 2048, F32)
        iota256 = s.av(M_IOTA256, 128, 256, F32)
        icol2 = s.av(M_ICOL, 128, 32)
        eo_sb = [[s.av(M_EO + (ei * 2 + hf) * 2048, 128, 1024) for hf in range(2)]
                 for ei in range(4)]

        b.gp_fence()
        gi = b.emit("gpsimd", lambda e: e.iota(
            iota_t[:], pattern=[[1, 2048]], base=0, channel_multiplier=0,
            allow_small_or_imprecise_dtypes=True))
        gi = b.emit("gpsimd", lambda e: e.iota(
            iota256[:], pattern=[[1, 256]], base=0, channel_multiplier=0,
            allow_small_or_imprecise_dtypes=True))
        gi = b.emit("gpsimd", lambda e: e.iota(
            icol2.rearrange("p (c two) -> p c two", two=2)[:, :, 0:1],
            pattern=[[0, 16], [0, 1]], base=0, channel_multiplier=1,
            allow_small_or_imprecise_dtypes=True))
        gi = b.emit("gpsimd", lambda e: e.iota(
            icol2.rearrange("p (c two) -> p c two", two=2)[:, :, 1:2],
            pattern=[[128, 16], [0, 1]], base=0, channel_multiplier=0,
            allow_small_or_imprecise_dtypes=True), inc=s.GS)
        b.wait("vector", gi)
        b.wait("tensor", gi)

        for e0 in range(0, self.nexp, 4):
            for ei in range(4):
                ex = e0 + ei
                self.expert(ex, ei, perm, xtok, h_e, idx_all, w_sb, eo_sb,
                            markers, iota256, icol2, wsA, wsB, wsd_b)
            self.scatter4(e0, pcm, iota_t, idx_all, eo_sb)
        b.join()
        s.dbg_moe = {"mcols": [s.m_cols, s.r_cols], "sctm": [s.scores_tm], "xtok": xtok, "h_e": h_e, "perm": [perm],
                     "markers": [markers], "eo": [t for p in eo_sb for t in p],
                     "sworkv": [s.scores_work], "idx": [idx_all]}


    def expert(self, ex, ei, perm, xtok, h_e, idx_all, w_sb, eo_sb,
               markers, iota256, icol2, wsA, wsB, wsd_b):
        b, s = self.b, self
        dram = self.dram
        b.wait("vector", self._perm_g)
        ptok = None
        for tc in range(TC):
            mk = markers[:, tc * 16 + ex: tc * 16 + ex + 1]
            ptok = b.emit("vector", lambda e, tc=tc, mk=mk: e.tensor_scalar(
                perm[:, tc * 256:(tc + 1) * 256], iota256[:], mk, None,
                ALU.is_equal), inc=(s.VS if tc == TC - 1 else None))
        b.wait("tensor", ptok)
        for dc in range(DC):
            bk = dc % 2
            psg = self.PS[0:128, bk * 512: bk * 512 + 256]
            b.wait("tensor", self._g_g.get(bk))
            mt = None
            for tc in range(TC):
                mt = b.emit("tensor", lambda e, psg=psg, tc=tc, dc=dc: e.matmul(
                    psg, s.xln_tm[tc][:, ts(dc)], perm[:, tc * 256:(tc + 1) * 256],
                    start=(tc == 0), stop=(tc == TC - 1)),
                    inc=(s.TS if tc == TC - 1 else None))
            b.wait("scalar", mt)
            st = b.emit("scalar", lambda e, psg=psg, dc=dc: e.copy(
                xtok[dc][:], psg), inc=s.SS)
            self._g_g[bk] = st
        b.wait("tensor", self._g_g.get(0))
        b.wait("tensor", self._g_g.get(1))
        b.wait("tensor", self._up_g.get(0))
        b.wait("tensor", self._up_g.get(1))
        b.wait("vector", self._idx_g)
        for hf in range(2):
            ps_i = self.PS[0:128, hf * 1024: hf * 1024 + 1]
            ps_w = self.PS[0:128, hf * 1024 + 512: hf * 1024 + 513]
            mw = None
            first = True
            for tc in range(TC):
                lh = perm[:, tc * 256 + hf * 128: tc * 256 + (hf + 1) * 128]
                for c2 in range(2):
                    b.emit("tensor", lambda e, ps_i=ps_i, lh=lh, tc=tc, c2=c2,
                           st=first: e.matmul(
                        ps_i, lh, icol2[:, tc * 2 + c2: tc * 2 + c2 + 1],
                        start=st, stop=(tc == TC - 1 and c2 == 1)))
                    first = False
                mw = b.emit("tensor", lambda e, ps_w=ps_w, lh=lh, tc=tc, ex=ex: e.matmul(
                    ps_w, lh, s.scores_tm_bf[:, tc * 16 + ex: tc * 16 + ex + 1],
                    start=(tc == 0), stop=(tc == TC - 1)),
                    inc=(s.TS if tc == TC - 1 else None))
            b.wait("scalar", mw)
            vt = b.emit("scalar", lambda e, ps_i=ps_i, ei=ei, hf=hf: e.copy(
                idx_all[:, ei * 2 + hf: ei * 2 + hf + 1], ps_i), inc=s.SS)
            sw = b.emit("scalar", lambda e, ps_w=ps_w, hf=hf: e.copy(
                w_sb[:, hf:hf + 1], ps_w), inc=s.SS)
            self._w_g[hf] = sw
            self._idxv = vt
            self._g_g[2 * hf] = sw
            self._g_g[2 * hf + 1] = sw
        b.wait("tensor", self._g_g.get(2))
        b.wait("tensor", self._g_g.get(3))
        for q in range(4):
            p = self._wgrp % 2
            self._wgrp += 1
            wA, wB = s.wsp[p]
            semA, semB = s.wsp_sem[p]
            b.wait("gpsimd", self._wpair_g.get(p))
            wtA = b.emit("gpsimd", lambda e, q=q, ex=ex, wA=wA: e.dma_start(
                out=wA[:], in_=dram["wg_s"][ex, q]), inc=semA, by=16)
            wtB = b.emit("gpsimd", lambda e, q=q, ex=ex, wB=wB: e.dma_start(
                out=wB[:], in_=dram["wu_s"][ex, q]), inc=semB, by=16)
            b.wait("tensor", wtA)
            b.wait("tensor", wtB)
            uplast = None
            for fsq in range(4):
                fs = q * 4 + fsq
                psg = self.PS[0:128, (2 + fs % 2) * 512: (2 + fs % 2) * 512 + 256]
                psu = self.PS[0:128, (4 + fs % 2) * 512: (4 + fs % 2) * 512 + 256]
                b.wait("tensor", self._up_g.get(fs % 2))
                for dc in range(DC):
                    b.emit("tensor", lambda e, psg=psg, fsq=fsq, dc=dc, wA=wA: e.matmul(
                        psg, wA[:, (fsq * 8 + dc) * 128:(fsq * 8 + dc + 1) * 128],
                        xtok[dc][:], start=(dc == 0), stop=(dc == DC - 1)))
                    uplast = b.emit("tensor", lambda e, psu=psu, fsq=fsq, dc=dc, wB=wB: e.matmul(
                        psu, wB[:, (fsq * 8 + dc) * 128:(fsq * 8 + dc + 1) * 128],
                        xtok[dc][:], start=(dc == 0), stop=(dc == DC - 1)),
                        inc=(s.TS if dc == DC - 1 else None))
                b.wait("scalar", uplast)
                tgl = b.emit("scalar", lambda e, psg=psg, fs=fs: e.activation(
                    h_e[fs][:], psg, AF.Gelu), inc=s.SS)
                b.wait("vector", tgl)
                hv = b.emit("vector", lambda e, psu=psu, fs=fs: e.tensor_tensor(
                    h_e[fs][:], h_e[fs][:], psu, ALU.mult), inc=s.VS)
                self._up_g[fs % 2] = hv
            self._wpair_g[p] = uplast
        ps_eo = [self.PS[0:128, 2048:3072], self.PS[0:128, 3072:4096]]
        b.wait("tensor", self._eo_g)
        b.wait("tensor", self._up_g.get(0))
        b.wait("tensor", self._up_g.get(1))
        dlast = None
        for q in range(4):
            p = self._wgrp % 2
            self._wgrp += 1
            wq = s.wsp[p][0]
            b.wait("gpsimd", self._wpair_g.get(p))
            wt = b.emit("gpsimd", lambda e, q=q, ex=ex, wq=wq: e.dma_start(
                out=wq[:], in_=dram["wd_s"][ex, q]),
                inc=s.wsp_sem[p][0], by=16)
            b.wait("tensor", wt)
            for fsq in range(4):
                fs = q * 4 + fsq
                b.wait("tensor", self._up_g.get(fs % 2))
                for hf in range(2):
                    for dh in range(2):
                        dlast = b.emit("tensor", lambda e, fs=fs, fsq=fsq, hf=hf, dh=dh, wq=wq: e.matmul(
                            ps_eo[hf][:, dh * 512:(dh + 1) * 512],
                            h_e[fs][:, hf * 128:(hf + 1) * 128],
                            wq[:, fsq * 1024 + dh * 512: fsq * 1024 + (dh + 1) * 512],
                            start=(fs == 0), stop=(fs == 15)),
                            inc=(s.TS if fsq == 3 and hf == 1 and dh == 1 else None))
            self._wpair_g[p] = dlast
        b.wait("scalar", dlast)
        b.wait("scalar", self._eo_scatter_g)
        ev = None
        for hf in range(2):
            b.wait("scalar", self._w_g.get(hf))
            ev = b.emit("scalar", lambda e, hf=hf, ei=ei: e.activation(
                eo_sb[ei][hf][:], ps_eo[hf], AF.Copy, scale=w_sb[:, hf:hf + 1]),
                inc=s.SS)
        self._eo_g = ev
        self._perm_g = dlast

    def scatter4(self, e0, pcm, iota_t, idx_all, eo_sb):
        b, s = self.b, self
        b.wait("tensor", self._eo_g)
        b.wait("vector", self._idxv)
        for tt in range(NT):
            col = slice(tt * 512, (tt + 1) * 512)
            ptoks = []
            for j in range(8):
                b.wait("vector", self._pcm_g.get(j))
                ptoks.append(b.emit("vector", lambda e, j=j, tt=tt: e.tensor_scalar(
                    pcm[j][:], iota_t[:, tt * 512:(tt + 1) * 512],
                    idx_all[:, j:j + 1], None, ALU.is_equal), inc=s.VS))
            for tok in ptoks:
                b.wait("tensor", tok)
            for dc in range(DC):
                bk = (tt * DC + dc) % 2
                pss = self.bank(bk)
                b.wait("tensor", self._sc_g.get(bk))
                mt = None
                for j in range(8):
                    ei, hf = j // 2, j % 2
                    mt = b.emit("tensor", lambda e, pss=pss, ei=ei, hf=hf, j=j, dc=dc: e.matmul(
                        pss, eo_sb[ei][hf][:, ts(dc)], pcm[j][:],
                        start=(j == 0), stop=(j == 7)),
                        inc=(s.TS if j == 7 else None))
                if dc == DC - 1:
                    for j in range(8):
                        self._pcm_g[j] = mt
                b.wait("vector", mt)
                vt = b.emit("vector", lambda e, pss=pss, dc=dc, col=col: e.tensor_add(
                    s.x_fm[dc][:, col], s.x_fm[dc][:, col], pss), inc=s.VS)
                self._sc_g[bk] = vt
        self._eo_scatter_g = mt
        self._idx_g = ptoks[-1]
        for bk in (0, 1):
            if self._sc_g.get(bk) is not None:
                self._g_g[bk] = self._sc_g[bk]


    def l1ffn(self):
        b, s = self.b, self
        dram = self.dram
        F_H = 32 * 1024
        F_WS = 96 * 1024
        F_SCR = 114688
        h_t = [s.av(F_H + j * 2048, 128, 1024) for j in range(HID1 // 128)]
        wsA = s.av(F_WS, 128, 4096)
        wsB = s.av(F_WS + 8192, 128, 4096)
        absb = s.av(F_SCR, 128, 4096)
        tmpf = s.av(F_SCR + 8192, 128, 512, F32)
        arow = s.av(F_SCR + 10240, 1, 128, F32)

        b.gp_fence()
        b.wait("gpsimd", (s.SCONST, b.semval[s.SCONST]))
        bt1 = b.emit("gpsimd", lambda e: e.dma_start(
            out=s.b1col[:], in_=dram["b1_pt"]), inc=s.SCONST, by=16)
        b.wait("gpsimd", (s.SCONST, b.semval[s.SCONST]))
        bt2 = b.emit("gpsimd", lambda e: e.dma_start(
            out=s.b2col[:], in_=dram["b2_pt"]), inc=s.SCONST, by=16)
        b.wait("scalar", bt1)
        b.wait("scalar", bt2)

        wg = {}
        sg = {}
        for ph in range(2):
            tb = ph * 1024
            for j in range(HID1 // 128):
                half = j % 2
                wbuf = s.wbuf[half]
                b.wait("gpsimd", wg.get(half))
                wt = b.emit("gpsimd", lambda e, j=j, wbuf=wbuf: e.dma_start(
                    out=wbuf[:, 0:1024], in_=dram["w1_s"][j]), inc=s.SW[half], by=16)
                b.wait("scalar", wt)
                sdst = wsA if half == 0 else wsB
                b.wait("scalar", sg.get(half))
                if ph == 0:
                    ta = b.emit("scalar", lambda e, wbuf=wbuf: e.activation(
                        absb[:, 0:1024], wbuf[:, 0:1024], AF.Abs), inc=s.SS)
                tsg = b.emit("scalar", lambda e, wbuf=wbuf, sdst=sdst: e.sign(
                    sdst[:, 0:1024], wbuf[:, 0:1024]), inc=s.SS)
                wg[half] = tsg
                b.wait("tensor", tsg)
                for t2 in range(2):
                    col = slice(tb + t2 * 512, tb + (t2 + 1) * 512)
                    bk = (j * 2 + t2) % 2
                    psh = self.bank(bk)
                    b.wait("tensor", self._l1_g.get(bk))
                    mt = None
                    for dc in range(DC):
                        mt = b.emit("tensor", lambda e, psh=psh, sdst=sdst, dc=dc, col=col: e.matmul(
                            psh, sdst[:, dc * 128:(dc + 1) * 128], s.xln[dc][:, col],
                            start=(dc == 0), stop=(dc == DC - 1)),
                            inc=(s.TS if dc == DC - 1 else None))
                    sg[half] = mt
                    if t2 == 0 and ph == 0:
                        b.wait("tensor", ta)
                        b.wait("tensor", self._l1a_g)
                        psa = self.PS[0:1, 1024:1536]
                        tm = None
                        for hh in range(2):
                            tm = b.emit("tensor", lambda e, psa=psa, hh=hh: e.matmul(
                                psa, s.ones[:, 0:1], absb[:, hh * 512:(hh + 1) * 512],
                                start=(hh == 0), stop=(hh == 1)),
                                inc=(s.TS if hh == 1 else None))
                        b.wait("vector", tm)
                        vred = b.emit("vector", lambda e, psa=psa: e.tensor_reduce(
                            self.PS[0:1, 1536:1664],
                            psa.rearrange("p (a c) -> p c a", a=4), AX.X, ALU.add),
                            inc=s.VS)
                        b.wait("scalar", vred)
                        tsc = b.emit("scalar", lambda e: e.activation(
                            arow[:], self.PS[0:1, 1536:1664], AF.Copy,
                            scale=1.0 / D), inc=s.SS)
                        b.wait("tensor", tsc)
                        tcl = b.emit("tensor", lambda e: e.matmul(
                            self.PS[0:128, 1664:1665], arow[:], s.ones_f[0:1, 0:1],
                            start=True, stop=True), inc=s.TS)
                        b.wait("scalar", tcl)
                        ac = b.emit("scalar", lambda e, j=j: e.copy(
                            s.a1col[:, j:j + 1], self.PS[0:128, 1664:1665]), inc=s.SS)
                        b.drain("scalar")
                        self._l1a_g = ac
                    b.wait("scalar", mt)
                    hcol = slice(t2 * 512, (t2 + 1) * 512)
                    ht = b.emit("scalar", lambda e, psh=psh, j=j, hcol=hcol: e.activation(
                        h_t[j][:, hcol], psh, AF.Gelu, scale=s.a1col[:, j:j + 1],
                        bias=s.b1col[:, j:j + 1]), inc=s.SS)
                    self._l1_g[bk] = ht
            for dcb in range(DC):
                half = dcb % 2
                wbuf2 = wsA if half == 0 else wsB
                b.wait("gpsimd", wg.get(2 + half))
                b.wait("gpsimd", sg.get(half))
                wt = b.emit("gpsimd", lambda e, dcb=dcb, wbuf2=wbuf2: e.dma_start(
                    out=wbuf2, in_=dram["w2_s"][dcb]),
                    inc=(s.SA if half == 0 else s.SB), by=16)
                b.wait("scalar", wt)
                if ph == 0:
                    ta = b.emit("scalar", lambda e, wbuf2=wbuf2: e.activation(
                        absb[:], wbuf2[:], AF.Abs), inc=s.SS)
                tsg = b.emit("scalar", lambda e, wbuf2=wbuf2: e.sign(
                    wbuf2[:], wbuf2[:]), inc=s.SS)
                wg[2 + half] = tsg
                b.wait("tensor", tsg)
                for t2 in range(2):
                    col = slice(tb + t2 * 512, tb + (t2 + 1) * 512)
                    hcol = slice(t2 * 512, (t2 + 1) * 512)
                    bk = (dcb * 2 + t2) % 2
                    psd = self.bank(bk)
                    b.wait("tensor", self._l1_g.get(bk))
                    mt = None
                    for j in range(HID1 // 128):
                        mt = b.emit("tensor", lambda e, psd=psd, wbuf2=wbuf2, j=j, hcol=hcol: e.matmul(
                            psd, wbuf2[:, j * 128:(j + 1) * 128], h_t[j][:, hcol],
                            start=(j == 0), stop=(j == HID1 // 128 - 1)),
                            inc=(s.TS if j == HID1 // 128 - 1 else None))
                    sg[half] = mt
                    if t2 == 0 and ph == 0:
                        b.wait("tensor", ta)
                        b.wait("tensor", self._l1a_g)
                        psa = self.PS[0:1, 1024:1536]
                        tm = None
                        for hh in range(8):
                            tm = b.emit("tensor", lambda e, psa=psa, hh=hh: e.matmul(
                                psa, s.ones[:, 0:1], absb[:, hh * 512:(hh + 1) * 512],
                                start=(hh == 0), stop=(hh == 7)),
                                inc=(s.TS if hh == 7 else None))
                        b.wait("vector", tm)
                        vred = b.emit("vector", lambda e, psa=psa: e.tensor_reduce(
                            self.PS[0:1, 1536:1664],
                            psa.rearrange("p (a c) -> p c a", a=4), AX.X, ALU.add),
                            inc=s.VS)
                        b.wait("scalar", vred)
                        tsc = b.emit("scalar", lambda e: e.activation(
                            arow[:], self.PS[0:1, 1536:1664], AF.Copy,
                            scale=1.0 / HID1), inc=s.SS)
                        b.wait("tensor", tsc)
                        tcl = b.emit("tensor", lambda e: e.matmul(
                            self.PS[0:128, 1664:1665], arow[:], s.ones_f[0:1, 0:1],
                            start=True, stop=True), inc=s.TS)
                        b.wait("scalar", tcl)
                        ac = b.emit("scalar", lambda e, dcb=dcb: e.copy(
                            s.a2col[:, dcb:dcb + 1], self.PS[0:128, 1664:1665]), inc=s.SS)
                        self._l1a_g = ac
                    b.wait("vector", self._l1a_g)
                    b.wait("vector", mt)
                    et = b.emit("vector", lambda e, psd=psd, dcb=dcb: e.tensor_scalar(
                        tmpf[:], psd, s.a2col[:, dcb:dcb + 1], s.b2col[:, dcb:dcb + 1],
                        ALU.mult, ALU.add), inc=s.VS)
                    b.drain("vector")
                    vt = b.emit("vector", lambda e, dcb=dcb, col=col: e.tensor_add(
                        s.x_fm[dcb][:, col], s.x_fm[dcb][:, col], tmpf[:]), inc=s.VS)
                    self._l1_g[bk] = vt
                    b.drain("vector")
            b.join()
        b.join()

    def dump(self, tiles, n, cols, pos=0):
        b, s = self.b, self
        tiles = [t[:] if hasattr(t, "alloc_sbuf_tensor") or not hasattr(t, "ap") or callable(getattr(t, "ap")) else t for t in tiles]
        jt = b.join()
        for eng, tok in jt.items():
            b.wait("gpsimd", tok)
        for i in range(min(n, len(tiles))):
            if pos + cols > 16384:
                return
            nparts = tiles[i].ap[0][1]
            b.emit("gpsimd", lambda e, i=i, pos=pos, cols=cols, nparts=nparts: e.dma_start(
                out=self.dbg_t[0:nparts, pos:pos + cols], in_=tiles[i][:, 0:cols]),
                inc=s.STG, by=16)
            pos += cols
        b.wait("gpsimd", (s.STG, b.semval[s.STG]))

    def store_x(self):
        b, s = self.b, self
        jt = b.join()
        for eng, tok in jt.items():
            b.wait("sync", tok)
        outv = self.out.rearrange("(i p) t -> p i t", p=128)
        inv = s.x_all[:].rearrange("p (i t) -> p i t", i=DC)
        b.emit("sync", lambda e: e.dma_start(out=outv, in_=inv), inc=s.ST, by=16)
        b.wait("sync", (s.ST, self.b.semval[s.ST]))


_prog_init = Prog.__init__
def _prog_init2(self, *a, **kw):
    _prog_init(self, *a, **kw)
    self._ln_g = {}
    self._ln_ps_g = None
    self._ln_bc_g = None
    self._sh_g = {}
    self._wsd_g = {}
    self._shw_g = None
    self._mrt_g = {}
    self._gate_tex = None
    self._l1_g = {}
    self._l1a_g = None
    self._idxv = None
    self._perm_g = None
    self._g_g = {}
    self._idx_g = None
    self._w_g = {}
    self._wsAB_g = None
    self._up_g = {}
    self._eo_g = None
    self._eo_scatter_g = None
    self._pcm_g = {}
    self._sc_g = {}
Prog.__init__ = _prog_init2



def _const_arrays():
    t = np.arange(T, dtype=np.float64)[:, None]
    inv = 1.0 / (10000.0 ** (np.arange(0, DH, 2, dtype=np.float64) / DH))
    fr = t * inv[None, :]
    cos = np.cos(fr).astype(np.float32)
    sin = np.sin(fr).astype(np.float32)

    def pack(a):
        return np.ascontiguousarray(
            a.reshape(TC, 128, 32).transpose(1, 0, 2).reshape(128, TC * 32))
    cs = np.concatenate([pack(cos), pack(sin)], axis=1)
    return {"cs_tab": cs, "ident": np.eye(128, dtype=np.float32)}


def _qkv_tiles(w):
    return np.ascontiguousarray(
        w.reshape(DC, 128, 12, 256).transpose(2, 1, 0, 3).reshape(12, 128, 2048))


def prep_inputs(inputs):
    f32 = np.float32
    A = np.ascontiguousarray
    g = dict(_const_arrays())
    g["wqkv_s"] = _qkv_tiles(np.asarray(inputs["L0_attn_W_qkv"], f32))
    g["bqkv"] = A(np.asarray(inputs["L0_attn_b_qkv"], f32).reshape(1, 3 * D))
    g["wout0"] = A(np.asarray(inputs["L0_attn_W_out"], f32).reshape(DC, 128, D))
    g["bout0_pt"] = A(np.asarray(inputs["L0_attn_b_out"], f32).reshape(DC, 128).T)
    gw = np.asarray(inputs["L0_ffn_gate_w"], f32).T
    gp = np.zeros((128, 128), f32)
    gp[:, :DC * E] = gw.reshape(DC, 128, E).transpose(1, 0, 2).reshape(128, DC * E)
    g["gate_pk"] = A(gp)
    g["lt128"] = A(np.triu(np.ones((128, 128), f32)))
    wg = np.asarray(inputs["L0_ffn_Wg"], f32)
    wu = np.asarray(inputs["L0_ffn_Wu"], f32)
    wd = np.asarray(inputs["L0_ffn_Wd"], f32)
    def up_tiles(w):
        wt = w.transpose(0, 2, 1)
        return A(wt.reshape(E, DC, 128, 16, 128)
                 .transpose(0, 3, 2, 1, 4)
                 .reshape(E, 16, 128, DC * 128)
                 .reshape(E, 4, 4, 128, 1024)
                 .transpose(0, 1, 3, 2, 4)
                 .reshape(E, 4, 128, 4096))
    g["wg_s"] = up_tiles(wg)
    g["wu_s"] = up_tiles(wu)
    wdt = wd.transpose(0, 2, 1)
    g["wd_s"] = A(wdt.reshape(E, 4, 4, 128, D).transpose(0, 1, 3, 2, 4)
                  .reshape(E, 4, 128, 4096))
    wsg = np.asarray(inputs["L0_ffn_Ws_g"], f32)
    wsu = np.asarray(inputs["L0_ffn_Ws_u"], f32)
    wsd = np.asarray(inputs["L0_ffn_Ws_d"], f32)
    def up1_tiles(w):
        wt = w.T
        return A(wt.reshape(DC, 128, 16, 128).transpose(2, 1, 0, 3)
                 .reshape(16, 128, DC * 128)
                 .reshape(4, 4, 128, 1024).transpose(0, 2, 1, 3)
                 .reshape(4, 128, 4096))
    g["wsg_s"] = up1_tiles(wsg)
    g["wsu_s"] = up1_tiles(wsu)
    g["wsd_s"] = A(wsd.T.reshape(FSH // 128, 128, D))
    wq = np.asarray(inputs["L1_attn_W_Q"], f32).transpose(2, 0, 1).reshape(D, D)
    wk = np.asarray(inputs["L1_attn_W_K"], f32).transpose(2, 0, 1).reshape(D, D)
    wv = np.asarray(inputs["L1_attn_W_V"], f32).transpose(2, 0, 1).reshape(D, D)
    g["wq1_s"] = _qkv_tiles(np.concatenate([wq, wk, wv], axis=1))
    bq = np.asarray(inputs["L1_attn_b_Q"], f32).reshape(D)
    bk = np.asarray(inputs["L1_attn_b_K"], f32).reshape(D)
    bv = np.asarray(inputs["L1_attn_b_V"], f32).reshape(D)
    g["bqkv1"] = A(np.concatenate([bq, bk, bv]).reshape(1, 3 * D))
    g["wout1"] = A(np.asarray(inputs["L1_attn_W_O"], f32)
                   .transpose(0, 2, 1).reshape(D, D).reshape(DC, 128, D))
    g["bout1_pt"] = A(np.asarray(inputs["L1_attn_b_O"], f32).reshape(DC, 128).T)
    w1 = np.asarray(inputs["L1_ffn_w1"], f32)
    w2 = np.asarray(inputs["L1_ffn_w2"], f32)
    g["w1_s"] = A(w1.reshape(HID1 // 128, 128, DC, 128)
                  .transpose(0, 3, 2, 1).reshape(HID1 // 128, 128, D))
    g["b1_pt"] = A(np.asarray(inputs["L1_ffn_b1"], f32).reshape(HID1 // 128, 128).T)
    g["w2_s"] = A(w2.reshape(DC, 128, HID1 // 128, 128)
                  .transpose(0, 3, 2, 1).reshape(DC, 128, HID1))
    g["b2_pt"] = A(np.asarray(inputs["L1_ffn_b2"], f32).reshape(DC, 128).T)
    xs = [A(np.asarray(inputs["L0_attn_x"][c], f32).T
            .reshape(DC, 128, T).transpose(1, 0, 2).reshape(128, DC * T))
          for c in range(8)]
    return g, xs


def kernel(**inputs):
    g, xs = prep_inputs(inputs)
    nc = Prog().build()
    in_maps = [dict(g, xT=x) for x in xs]
    res = run_bass_kernel_spmd(nc, in_maps, core_ids=list(range(8)))
    out = np.stack([np.asarray(r["outT"]).T for r in res.results])
    return np.ascontiguousarray(out.astype(np.float32))
